# version 08
# baseline (speedup 1.0000x reference)
import numpy as np
import ml_dtypes
import concourse.bass as bass
import concourse.mybir as mybir
from concourse.bass_utils import run_bass_kernel_spmd

F32 = mybir.dt.float32
BF16 = mybir.dt.bfloat16
U8 = mybir.dt.uint8
AF = mybir.ActivationFunctionType
ALU = mybir.AluOpType

SEQ = 8192
D = 1024
NSEG = 16
LSEG = 512
NH = 16
DK = 64
NEXP = 8
HID = 2048
TGRP = 1024
NGRP = SEQ // TGRP
NTT = TGRP // 128
NTC = TGRP // 512

EXP_SHIFT = -30.0


class _Op:
    __slots__ = ("stream", "fn", "deps", "dma", "inc", "tag")

    def __init__(self, stream, fn, deps, dma, tag):
        self.stream = stream
        self.fn = fn
        self.deps = deps
        self.dma = dma
        self.inc = False
        self.tag = tag


class Sched:
    STREAMS = ("pe", "vec", "act", "pool", "sync")

    def __init__(self, nc):
        self.nc = nc
        self.ops = []
        self.writer = {}
        self.readers = {}
        self.barrier_deps = ()

    def op(self, stream, fn, reads=(), writes=(), dma=False, tag=None):
        idx = len(self.ops)
        deps = set(self.barrier_deps)
        for k in reads:
            w = self.writer.get(k)
            if w is not None:
                deps.add(w)
        for k in writes:
            w = self.writer.get(k)
            if w is not None:
                deps.add(w)
            deps.update(self.readers.get(k, ()))
        deps.discard(idx)
        best = {}
        keep = []
        for d in deps:
            o = self.ops[d]
            if o.stream == "pe" and not o.dma and stream == "pe" and not dma:
                continue
            if o.dma:
                keep.append(d)
                continue
            key = o.stream
            if best.get(key, -1) < d:
                best[key] = d
        self.ops.append(_Op(stream, fn, tuple(sorted(best.values()) + keep),
                            dma, tag))
        for k in reads:
            self.readers.setdefault(k, []).append(idx)
        for k in writes:
            self.writer[k] = idx
            self.readers[k] = [idx]
        return idx

    def barrier(self):
        last = {}
        lastd = {}
        for i, o in enumerate(self.ops):
            if o.dma:
                lastd.setdefault(o.stream, []).append(i)
            else:
                last[(o.stream,)] = i
        deps = list(last.values())
        for q, lst in lastd.items():
            deps.extend(lst[-self.NLANES:])
        self.barrier_deps = tuple(deps)

    NLANES = 10

    def finalize_and_emit(self, sems):
        for o in self.ops:
            if o.dma:
                o.inc = True
            for d in o.deps:
                self.ops[d].inc = True
        pos = {}
        semkey_of = {}
        cnt = {}
        qidx = {}
        for i, o in enumerate(self.ops):
            if not o.inc:
                continue
            if o.dma:
                qi = qidx.get(o.stream, 0)
                qidx[o.stream] = qi + 1
                k = (o.stream, True, qi % self.NLANES)
                pos[i] = qi // self.NLANES + 1
            else:
                k = (o.stream, False)
                cnt[k] = cnt.get(k, 0) + 1
                pos[i] = cnt[k]
            semkey_of[i] = k
        self.pos = pos
        self.semkey_of = semkey_of
        final = dict(cnt)
        for i, o in enumerate(self.ops):
            if o.inc and o.dma:
                k = semkey_of[i]
                final[k] = max(final.get(k, 0), pos[i])
        self.final_counts = final

        per_stream = {s: [] for s in self.STREAMS}
        for i, o in enumerate(self.ops):
            per_stream[o.stream].append(i)

        def emit_stream(stream, eng):
            waited = {}

            def wait_for(dep):
                do = self.ops[dep]
                semkey = self.semkey_of[dep]
                val = self.pos[dep] * (16 if do.dma else 1)
                if waited.get(semkey, 0) >= val:
                    return
                eng.wait_ge(sems[semkey], val)
                waited[semkey] = val

            for i in per_stream[stream]:
                o = self.ops[i]
                for d in o.deps:
                    wait_for(d)
                if o.dma:
                    semkey = self.semkey_of[i]
                    gate = (self.pos[i] - 1) * 16
                    if gate > 0 and waited.get(semkey, 0) < gate:
                        eng.wait_ge(sems[semkey], gate)
                        waited[semkey] = gate
                inst = o.fn(eng)
                if o.inc:
                    semkey = self.semkey_of[i]
                    inst.then_inc(sems[semkey], 16 if o.dma else 1)
            if stream == "sync":
                for semkey, c in self.final_counts.items():
                    if len(semkey) == 3:
                        eng.wait_ge(sems[semkey], 16 * c)

        nc = self.nc
        with nc.Block() as block:
            @block.sync
            def _(eng):
                emit_stream("sync", eng)

            @block.tensor
            def _(eng):
                emit_stream("pe", eng)

            @block.vector
            def _(eng):
                emit_stream("vec", eng)

            @block.scalar
            def _(eng):
                emit_stream("act", eng)

            @block.gpsimd
            def _(eng):
                emit_stream("pool", eng)


def build_program(n_segs=NSEG, n_groups=NGRP, dbg_x1_out=False, dbg_taps=None, dbg_raw_ao=False):
    nc = bass.Bass("TRN2", target_bir_lowering=False)

    Xd = nc.dram_tensor("x", [SEQ, D], F32, kind="ExternalInput")
    Wq = nc.dram_tensor("wq", [D, D], F32, kind="ExternalInput")
    Wk = nc.dram_tensor("wk", [D, D], F32, kind="ExternalInput")
    Wv = nc.dram_tensor("wv", [D, D], F32, kind="ExternalInput")
    Wo = nc.dram_tensor("wo", [D, D], F32, kind="ExternalInput")
    Betab = nc.dram_tensor("betab", [128, NH * DK], F32, kind="ExternalInput")
    Maug0 = nc.dram_tensor("maug0", [DK, NH, 65], F32, kind="ExternalInput")
    Scale = nc.dram_tensor("nsc", [128, 8], F32, kind="ExternalInput")
    Wr = nc.dram_tensor("wr", [D, NEXP], F32, kind="ExternalInput")
    Wu = nc.dram_tensor("wu", [NEXP, D, 2 * HID], F32, kind="ExternalInput")
    Wd = nc.dram_tensor("wd", [NEXP, HID, D], F32, kind="ExternalInput")
    Yd = nc.dram_tensor("y", [SEQ, D], F32, kind="ExternalOutput")
    x1_kind = "ExternalOutput" if dbg_x1_out else "Internal"
    X1d = nc.dram_tensor("x1", [SEQ, D], F32, kind=x1_kind)
    Vpd = nc.dram_tensor("vpd", [2, LSEG, D], BF16, kind="Internal")
    Wub = nc.dram_tensor("wub", [NEXP, D, 2 * HID], BF16, kind="Internal")
    Wdb = nc.dram_tensor("wdb", [NEXP, HID, D], BF16, kind="Internal")
    Skd = nc.dram_tensor("skd", [2, LSEG, D], BF16, kind="Internal")

    idn = nc.inline_tensor(np.eye(128, dtype=ml_dtypes.bfloat16), "idn")
    iotaC_np = np.broadcast_to(np.arange(224, dtype=np.float32)[None, :],
                               (128, 224)).copy()
    iotaCh = nc.inline_tensor(iotaC_np, "iotac")
    iota2_np = np.stack([np.arange(128, dtype=np.float32),
                         np.arange(128, dtype=np.float32) + 128.0], 1).copy()
    iota2h = nc.inline_tensor(iota2_np, "iota2")
    e8_np = np.zeros((8, 8, 128), np.float32)
    for _k in range(8):
        e8_np[_k, _k, :] = 1.0
    e8h = nc.inline_tensor(e8_np.astype(ml_dtypes.bfloat16), "e8b")
    tril = np.triu(np.ones((128, 128), np.float32)).astype(ml_dtypes.bfloat16)
    trilh = nc.inline_tensor(tril, "trilc")

    sbuf = lambda n, s, d=F32: nc.alloc_sbuf_tensor(n, s, d)
    id_sb = sbuf("id_sb", [128, 128], BF16)
    mk_sb = sbuf("mk_sb", [128, 128], BF16)
    cst = sbuf("cst", [128, 8], F32)
    G_sb = sbuf("G_sb", [128, NH, DK], BF16)
    betat = sbuf("betat", [128, NH * DK], F32)
    sc_sb = sbuf("sc_sb", [128, 8], F32)
    M_sb = sbuf("M_sb", [DK, NH, 65], F32)
    M2_sb = sbuf("M2_sb", [128, NH, 65], BF16)
    stats = sbuf("stats", [128, 4, 12], F32)
    mv = sbuf("mv", [128, 4, 2], F32)
    rst = sbuf("rst", [128, 4, 2], F32)
    wr_sb = sbuf("wr_sb", [128, 8, NEXP], BF16)
    att_sb = sbuf("att_sb", [128, 4, 4, DK], BF16)
    e0_sb = sbuf("e0_sb", [128, 2, 4, DK], BF16)
    e1_sb = sbuf("e1_sb", [128, 2, 4, DK], BF16)
    d_sb = sbuf("d_sb", [128, 2, 4, DK], BF16)
    rc_sb = sbuf("rc_sb", [128, 2, 2, 4], F32)
    sc8 = sbuf("sc8", [128, 8], F32)
    iotaC = sbuf("iotaCs", [128, 224], F32)
    iota2 = sbuf("iota2s", [128, 2], F32)
    e8_sb = sbuf("e8_sb", [8, 8, 128], BF16)
    t8 = sbuf("t8", [128, 8], F32)
    r8 = sbuf("r8", [128, 8], F32)

    ARENA = 190976
    AR = nc.alloc_sbuf_tensor("arena", [128, ARENA], U8)
    arena_base = nc.lookup_mloc(AR).addr

    def mkplacer():
        state = {"off": 0}

        def place(name, shape, dtype):
            per = int(np.prod(shape[1:])) * mybir.dt.size(dtype)
            off = (state["off"] + 31) & ~31
            state["off"] = off + per
            assert state["off"] <= ARENA, (name, state["off"])
            h = nc.virtual_sbuf_tensor(name, list(shape), dtype)
            nc._tensor(h.name, list(h.shape), dtype, type="SB")
            mloc = nc.lookup_mloc(h)
            mloc.allocated = True
            mloc.addr = arena_base + off
            return h
        return place

    pa = mkplacer()
    wq_sb = pa("wq_sb", [128, 8, D], BF16)
    wk_sb = pa("wk_sb", [128, 8, D], BF16)
    wv_sb = pa("wv_sb", [128, 8, D], BF16)
    wo2_sb = pa("wo2_sb", [128, 8, D], BF16)
    T1 = pa("T1", [128, LSEG, 16], BF16)
    T2 = pa("T2", [128, LSEG, 16], BF16)
    sqscr = pa("sqscr", [DK, 128, 16], BF16)
    lnT = pa("lnT", [128, 8, LSEG], BF16)
    ln_sb = pa("ln_sb", [128, 2, D], BF16)
    xsl = pa("xsl", [128, 2, D], F32)
    xsr = xsl
    es_sb = pa("es_sb", [128, 3, 4, LSEG], BF16)
    vtk = pa("vtk", [128, 2, 4, 4, 65], BF16)
    sktk = pa("sktk", [128, 2, 4, 4, DK], BF16)
    ktmp = pa("ktmp", [128, 2, D], BF16)
    sktmp = pa("sktmp", [128, 2, D], BF16)
    vp_sb = pa("vp_sb", [128, 2, D], BF16)
    AF2 = pa("AF2", [128, 8, LSEG], BF16)
    ao_sb = pa("ao_sb", [128, 2, D], F32)
    wstg = pa("wstg", [128, 2, 2048], BF16)

    pb_ = mkplacer()
    xn_tok = pb_("xn_tok", [128, 2, 8, D], BF16)
    xnTt = pb_("xnTt", [128, 1, 8, 128], BF16)
    xgT = pb_("xgT", [128, 2, 2, 8, 224], BF16)
    hg_sb = pb_("hg_sb", [128, 2, 16, 224], BF16)
    dg_sb = pb_("dg_sb", [128, 2, 2, D], BF16)
    STe = pb_("STe", [128, 4, 224], BF16)
    SSb = pb_("SSb", [128, 2, 2, 512], BF16)
    slot_sb = pb_("slot_sb", [128, 2, 8, 8], F32)
    cumA = pb_("cumA", [8, 1024], BF16)
    cumB = pb_("cumB", [8, 1024], BF16)
    mT0 = pb_("mT0", [8, 512], BF16)
    slotTb = pb_("slotTb", [8, 2, 1024], BF16)
    m8_sb = pb_("m8_sb", [128, 8], BF16)
    accb = pb_("accb", [128, 2, NTT, D], BF16)
    wu_sb = pb_("wu_sb", [128, 3, 8, 256], BF16)
    wd_sb = pb_("wd_sb", [128, 16, D], BF16)
    x1t = pb_("x1t", [128, 2, D], F32)
    yt_sb = pb_("yt_sb", [128, 2, D], F32)
    sg_sb = pb_("sg_sb", [128, 2, 224], BF16)
    s_sb = pb_("s_sb", [128, 1, D], F32)
    wm_sb = pb_("wm_sb", [128, 2, NTT, NEXP], F32)

    pb = [nc.alloc_psum_tensor(f"pb{i}", [128, 512], F32) for i in range(8)]
    pbb = [p.ap().bitcast(BF16) for p in pb]

    S = Sched(nc)
    OP = S.op
    PBK = [(f"pb{i}",) for i in range(8)]

    OP("sync", lambda e: e.dma_start(out=id_sb[:], in_=idn[:]),
       writes=[("id",)], dma=True)
    OP("sync", lambda e: e.dma_start(out=mk_sb[:], in_=trilh[:]),
       writes=[("mk",)], dma=True)
    OP("sync", lambda e: e.dma_start(out=betat[:], in_=Betab[:]),
       writes=[("betat",)], dma=True)
    OP("sync", lambda e: e.dma_start(out=M_sb[:, :, :], in_=Maug0[:]),
       writes=[("Msb",)], dma=True)
    OP("sync", lambda e: e.dma_start(out=sc_sb[:], in_=Scale[:]),
       writes=[("scs",)], dma=True)
    OP("sync", lambda e: e.dma_start(out=iotaC[:], in_=iotaCh[:]),
       writes=[("iotaC",)], dma=True)
    OP("sync", lambda e: e.dma_start(out=iota2[:], in_=iota2h[:]),
       writes=[("iota2",)], dma=True)
    OP("sync", lambda e: e.dma_start(out=e8_sb[:, :, :], in_=e8h[:]),
       writes=[("e8",)], dma=True)
    OP("pool", lambda e: e.dma_start(
        out=wq_sb[:, :, :], in_=Wq.ap().rearrange("(kc p) n -> p kc n", p=128)),
        writes=[("wqs",)], dma=True)
    OP("pool", lambda e: e.dma_start(
        out=wk_sb[:, :, :], in_=Wk.ap().rearrange("(kc p) n -> p kc n", p=128)),
        writes=[("wks",)], dma=True)
    OP("pool", lambda e: e.dma_start(
        out=wv_sb[:, :, :], in_=Wv.ap().rearrange("(kc p) n -> p kc n", p=128)),
        writes=[("wvs",)], dma=True)
    OP("pool", lambda e: e.dma_start(
        out=wo2_sb[:, :, :],
        in_=Wo.ap().rearrange("(p8 ce dp) n -> (ce dp) p8 n", p8=8, ce=2, dp=64)),
        writes=[("wos",)], dma=True)
    OP("pool", lambda e: e.dma_start(
        out=wr_sb[:, :, :], in_=Wr.ap().rearrange("(kc p) n -> p kc n", p=128)),
        writes=[("wrs",)], dma=True)

    OP("vec", lambda e: e.memset(cst[:, 0:1], EXP_SHIFT), writes=[("cst",)])
    OP("vec", lambda e: e.memset(cst[:, 1:2], 1e-5 * 2.0 ** -44), writes=[("cst",)])
    OP("vec", lambda e: e.memset(cst[:, 2:3], 1e-6 * 2.0 ** -40), writes=[("cst",)])
    OP("vec", lambda e: e.memset(cst[:, 3:4], -22.0 * 0.6931471805599453), writes=[("cst",)])
    OP("vec", lambda e: e.memset(cst[:, 4:5], -20.0 * 0.6931471805599453), writes=[("cst",)])
    OP("vec", lambda e: e.memset(vtk[:, :, :, :, 64:65], 1.0), writes=[("vtk1",)])
    OP("act", lambda e: e.activation(out=betat[:], in_=betat[:], func=AF.Exp,
                                     scale=-1.0),
       reads=[("betat",)], writes=[("betat",)])
    OP("vec", lambda e: e.tensor_scalar(out=betat[:], in0=betat[:], scalar1=1.0,
                                        scalar2=None, op0=ALU.add),
       reads=[("betat",)], writes=[("betat",)])
    def _grecip(e):
        with nc.allow_low_precision(reason="gate sigmoid output in bf16"):
            return e.reciprocal(G_sb.ap().rearrange("p a b -> p (a b)"),
                                betat[:])
    OP("vec", _grecip, reads=[("betat",)], writes=[("G",)])

    XdT = Xd.ap().rearrange("(t tt p) d -> t tt p d", t=NSEG, tt=4, p=128)
    VpdV = Vpd.ap().rearrange("b (h jt rr) (c dp) -> b (rr c) h jt dp",
                              h=NH, jt=4, rr=8, c=16, dp=64)
    SkdV = Skd.ap().rearrange("b (h jt rr) (c dp) -> b (rr c) h jt dp",
                              h=NH, jt=4, rr=8, c=16, dp=64)

    def ln_tile(xap, xkeys, par, out_ap, outkey):
        OP("vec", lambda e: e.bn_stats(stats[:, par, 0:6], xap[:, 0:512]),
           reads=list(xkeys), writes=[("st_a", par)])
        OP("vec", lambda e: e.bn_stats(stats[:, par, 6:12], xap[:, 512:1024]),
           reads=list(xkeys), writes=[("st_b", par)])
        OP("vec", lambda e: e.bn_aggr(mv[:, par, :], stats[:, par, :]),
           reads=[("st_a", par), ("st_b", par)], writes=[("mv", par)])
        OP("act", lambda e: e.activation(out=rst[:, par, 1:2], in_=mv[:, par, 1:2],
                                         func=AF.Ln, scale=2.0 ** -44,
                                         bias=cst[:, 1:2]),
           reads=[("mv", par), ("cst",)], writes=[("rt", par)])
        OP("act", lambda e: e.activation(out=rst[:, par, 0:1], in_=rst[:, par, 1:2],
                                         func=AF.Exp, scale=-0.5, bias=cst[:, 3:4]),
           reads=[("rt", par), ("cst",)], writes=[("rs", par)])
        OP("vec", lambda e: e.tensor_scalar(out=out_ap, in0=xap,
                                            scalar1=mv[:, par, 0:1],
                                            scalar2=rst[:, par, 0:1],
                                            op0=ALU.subtract, op1=ALU.mult),
           reads=list(xkeys) + [("mv", par), ("rs", par)], writes=[outkey])

    def emit_ln_prep(t):
        for i in range(4):
            ip = i % 2
            OP("sync", lambda e, t=t, i=i, ip=ip: e.dma_start(
                out=xsl[:, ip, :], in_=XdT[t, i]),
               writes=[("xsl", ip)], dma=True)
            ln_tile(xsl[:, ip, :], [("xsl", ip)], ip, ln_sb[:, ip, :], ("ln", ip))
            for kc in range(8):
                bk = 4 + (kc % 4)
                OP("pe", lambda e, kc=kc, ip=ip, bk=bk: e.transpose(
                    pbb[bk][:, 0:128], ln_sb[:, ip, kc * 128:(kc + 1) * 128],
                    id_sb[:]),
                   reads=[("ln", ip), ("id",), PBK[bk]], writes=[PBK[bk]])
                OP("vec", lambda e, i=i, kc=kc, bk=bk: e.tensor_copy(
                    lnT[:, kc, i * 128:(i + 1) * 128], pbb[bk][:, 0:128]),
                   reads=[PBK[bk]], writes=[("lnT", kc, i)])

    def emit_seg(t, nseg):
        px = t % 2
        OP("vec", lambda e: e.tensor_copy(M2_sb[64:128, :, :], M_sb[:, :, :]),
           reads=[("Msb",)], writes=[("M2",)])

        lnT_keys = [("lnT", kc, i) for kc in range(8) for i in range(4)]

        for tt in range(4):
            tp = tt % 2
            for dh in range(2):
                bk = dh
                for kc in range(8):
                    OP("pe", lambda e, tt=tt, dh=dh, kc=kc, bk=bk: e.matmul(
                        pb[bk][:, :], lnT[:, kc, tt * 128:(tt + 1) * 128],
                        wv_sb[:, kc, dh * 512:(dh + 1) * 512],
                        start=(kc == 0), stop=(kc == 7)),
                       reads=[("lnT", kc, tt), ("wvs",), PBK[bk]],
                       writes=[PBK[bk]])
                OP("vec", lambda e, dh=dh, bk=bk, tp=tp: e.tensor_copy(
                    vp_sb[:, tp, dh * 512:(dh + 1) * 512], pb[bk][:, :]),
                   reads=[PBK[bk]], writes=[("vp", tp, dh)])
            OP("sync", lambda e, tt=tt, tp=tp, px=px: e.dma_start(
                out=Vpd[px, tt * 128:(tt + 1) * 128, :], in_=vp_sb[:, tp, :]),
               reads=[("vp", tp, 0), ("vp", tp, 1)], writes=[("vpd", px, tt)],
               dma=True)
        vpd_keys = [("vpd", px, tt) for tt in range(4)]

        def load_vtk(hq):
            hp = hq % 2
            OP("sync", lambda e, hq=hq, hp=hp, px=px: e.dma_start(
                out=vtk[:, hp, :, :, 0:64],
                in_=VpdV[px, :, 4 * hq:4 * hq + 4, :, :]),
               reads=vpd_keys, writes=[("vtk", hp)], dma=True)

        for tt in range(4):
            tp = tt % 2
            for dh in range(2):
                bk = 2 + dh
                for kc in range(8):
                    OP("pe", lambda e, tt=tt, dh=dh, kc=kc, bk=bk: e.matmul(
                        pb[bk][:, :], lnT[:, kc, tt * 128:(tt + 1) * 128],
                        wk_sb[:, kc, dh * 512:(dh + 1) * 512],
                        start=(kc == 0), stop=(kc == 7)),
                       reads=[("lnT", kc, tt), ("wks",), PBK[bk]],
                       writes=[PBK[bk]])
                OP("vec", lambda e, dh=dh, bk=bk, tp=tp: e.tensor_copy(
                    ktmp[:, tp, dh * 512:(dh + 1) * 512], pb[bk][:, :]),
                   reads=[PBK[bk]], writes=[("ktmp", tp)])
            OP("vec", lambda e, tp=tp: e.tensor_scalar_min(
                sktmp[:, tp, :], ktmp[:, tp, :], 0.0),
               reads=[("ktmp", tp)], writes=[("sktmp", tp)])
            OP("act", lambda e, tp=tp: e.activation(
                out=sktmp[:, tp, :], in_=sktmp[:, tp, :], func=AF.Exp),
               reads=[("sktmp", tp)], writes=[("sktmp", tp)])
            OP("vec", lambda e, tp=tp: e.scalar_tensor_tensor(
                out=sktmp[:, tp, :], in0=ktmp[:, tp, :], scalar=0.0,
                in1=sktmp[:, tp, :], op0=ALU.max, op1=ALU.add),
               reads=[("ktmp", tp), ("sktmp", tp)], writes=[("sktmp", tp)])
            OP("sync", lambda e, tt=tt, tp=tp, px=px: e.dma_start(
                out=Skd[px, tt * 128:(tt + 1) * 128, :], in_=sktmp[:, tp, :]),
               reads=[("sktmp", tp)], writes=[("skd", px, tt)], dma=True)
        skd_keys = [("skd", px, tt) for tt in range(4)]

        def load_sktk(hq):
            hp = hq % 2
            OP("sync", lambda e, hq=hq, hp=hp, px=px: e.dma_start(
                out=sktk[:, hp, :, :, :],
                in_=SkdV[px, :, 4 * hq:4 * hq + 4, :, :]),
               reads=skd_keys, writes=[("sktk", hp)], dma=True)
        load_vtk(0)
        load_sktk(0)
        load_vtk(1)
        load_sktk(1)

        for (wsb, wkey, dst, dkey, bk0) in (
                (wq_sb, ("wqs",), T1, "T1q", 0), (wk_sb, ("wks",), T2, "T2k", 2)):
            for p8 in range(8):
                bk = bk0 + p8 % 2
                for kc in range(8):
                    OP("pe", lambda e, p8=p8, kc=kc, bk=bk, wsb=wsb, wkey=wkey:
                       e.matmul(pb[bk][:, :], wsb[:, kc, p8 * 128:(p8 + 1) * 128],
                                lnT[:, kc, :], start=(kc == 0), stop=(kc == 7)),
                       reads=lnT_keys + [wkey, PBK[bk]], writes=[PBK[bk]])
                OP("vec", lambda e, p8=p8, bk=bk, dst=dst: e.tensor_copy(
                    dst[0:64, :, 2 * p8:2 * p8 + 1],
                    pb[bk][0:64, :].unsqueeze(2)),
                   reads=[PBK[bk]], writes=[(dkey, 2 * p8)])
                OP("vec", lambda e, p8=p8, bk=bk, dst=dst: e.tensor_copy(
                    dst[0:64, :, 2 * p8 + 1:2 * p8 + 2],
                    pb[bk][64:128, :].unsqueeze(2)),
                   reads=[PBK[bk]], writes=[(dkey, 2 * p8 + 1)])
        t1q_keys = [("T1q", c) for c in range(16)]
        t2k_keys = [("T2k", c) for c in range(16)]

        for rc in range(4):
            sl = slice(rc * 128, (rc + 1) * 128)
            OP("vec", lambda e, sl=sl: e.tensor_scalar_min(
                sqscr[:, :, :], T1[0:64, sl, :], 0.0),
               reads=t1q_keys, writes=[("sqs",)])
            OP("act", lambda e: e.activation(
                out=sqscr[:, :, :], in_=sqscr[:, :, :], func=AF.Exp),
               reads=[("sqs",)], writes=[("sqs",)])
            OP("vec", lambda e, sl=sl: e.scalar_tensor_tensor(
                out=sqscr[:, :, :], in0=T1[0:64, sl, :], scalar=0.0,
                in1=sqscr[:, :, :], op0=ALU.max, op1=ALU.add),
               reads=t1q_keys + [("sqs",)], writes=[("sqs",)])
            OP("vec", lambda e, sl=sl, rc=rc: e.tensor_copy(
                T1[64:128, sl, :], sqscr[:, :, :]),
               reads=[("sqs",)], writes=[("T1sq", rc)])

        def scores_part(h):
            hq = h // 4
            h2 = h % 3
            if h % 4 == 0 and hq >= 2:
                load_vtk(hq)
                load_sktk(hq)
            for jt in range(4):
                bk = jt % 2
                n = 512 - 128 * jt
                OP("pe", lambda e, h=h, jt=jt, bk=bk, n=n: e.matmul(
                    pb[bk][:, 0:n],
                    T2[0:64, 32 * h + 8 * jt:32 * h + 8 * jt + 8, :]
                    .rearrange("p a b -> p (a b)"),
                    T1[0:64, 32 * h + 8 * jt:32 * h + 32, :]
                    .rearrange("p a b -> p (a b)"),
                    start=True, stop=True),
                   reads=t1q_keys + t2k_keys + [PBK[bk]], writes=[PBK[bk]])
                OP("act", lambda e, jt=jt, bk=bk, n=n, h2=h2: e.activation(
                    out=es_sb[:, h2, jt, 128 * jt:512], in_=pb[bk][:, 0:n],
                    func=AF.Exp, bias=cst[:, 0:1], scale=0.125),
                   reads=[PBK[bk], ("cst",)], writes=[("es", h2, jt)])
                OP("vec", lambda e, jt=jt, h2=h2: e.tensor_tensor(
                    out=es_sb[:, h2, jt, 128 * jt:128 * jt + 128],
                    in0=es_sb[:, h2, jt, 128 * jt:128 * jt + 128],
                    in1=mk_sb[:], op=ALU.mult),
                   reads=[("es", h2, jt), ("mk",)], writes=[("es", h2, jt)])

        def umem_part(h):
            hq, hi, hp = h // 4, h % 4, (h // 4) % 2
            h2 = h % 3
            hb = h % 2
            bU = 6 if hb == 0 else 4
            bM = 7 if hb == 0 else 5
            for it in range(4):
                for jt in range(it + 1):
                    OP("pe", lambda e, it=it, jt=jt, hp=hp, hi=hi, h2=h2,
                       bU=bU: e.matmul(
                        pb[bU][:, it * 65:(it + 1) * 65],
                        es_sb[:, h2, jt, 128 * it:128 * it + 128],
                        vtk[:, hp, hi, jt, :],
                        start=(jt == 0), stop=(jt == it)),
                       reads=[("es", h2, jt), ("vtk", hp), ("vtk1",), PBK[bU]],
                       writes=[PBK[bU]])
            for it in range(4):
                OP("pe", lambda e, h=h, it=it, bM=bM: e.matmul(
                    pb[bM][:, it * 65:(it + 1) * 65],
                    T1[64:128, 32 * h + 8 * it:32 * h + 8 * it + 8, :]
                    .rearrange("p a b -> p (a b)"),
                    M2_sb[64:128, h, :], start=True, stop=True),
                   reads=[("T1sq", (32 * h) // 128), ("M2",), PBK[bM]],
                   writes=[PBK[bM]])
            for jt in range(4):
                OP("pe", lambda e, jt=jt, hp=hp, hi=hi: e.matmul(
                    pb[2][0:64, 0:65],
                    sktk[:, hp, hi, jt, :], vtk[:, hp, hi, jt, :],
                    start=(jt == 0), stop=(jt == 3)),
                   reads=[("sktk", hp), ("vtk", hp), ("vtk1",), PBK[2]],
                   writes=[PBK[2]])
            OP("vec", lambda e, h=h: e.tensor_add(
                M_sb[:, h, :], M_sb[:, h, :], pb[2][0:64, 0:65]),
               reads=[PBK[2], ("Msb",)], writes=[("Msb",)])
            h2 = hb
            pU = pb[bU][:, 0:260].rearrange("p (a b) -> p a b", b=65)
            pM = pb[bM][:, 0:260].rearrange("p (a b) -> p a b", b=65)
            OP("vec", lambda e, pU=pU, h2=h2: e.reciprocal(
                rc_sb[:, h2, 0, :], pU[:, :, 64]),
               reads=[PBK[bU]], writes=[("rc", h2)])
            OP("vec", lambda e, pM=pM, h2=h2: e.reciprocal(
                rc_sb[:, h2, 1, :], pM[:, :, 64]),
               reads=[PBK[bM]], writes=[("rc", h2)])
            OP("vec", lambda e, pU=pU, h2=h2: e.tensor_mul(
                e0_sb[:, h2, :, :], pU[:, :, 0:64],
                rc_sb[:, h2, 0, :].unsqueeze(2).broadcast_to([128, 4, 64])),
               reads=[PBK[bU], ("rc", h2)], writes=[("e0", h2)])
            OP("vec", lambda e, pM=pM, h2=h2: e.tensor_mul(
                e1_sb[:, h2, :, :], pM[:, :, 0:64],
                rc_sb[:, h2, 1, :].unsqueeze(2).broadcast_to([128, 4, 64])),
               reads=[PBK[bM], ("rc", h2)], writes=[("e1", h2)])
            OP("vec", lambda e, h2=h2: e.tensor_sub(
                d_sb[:, h2, :, :], e1_sb[:, h2, :, :], e0_sb[:, h2, :, :]),
               reads=[("e0", h2), ("e1", h2)], writes=[("d", h2)])
            OP("vec", lambda e, h2=h2, h=h: e.tensor_mul(
                d_sb[:, h2, :, :], d_sb[:, h2, :, :],
                G_sb[:, h, :].unsqueeze(1).broadcast_to([128, 4, 64])),
               reads=[("d", h2), ("G",)], writes=[("d", h2)])
            h4 = h % 4
            OP("vec", lambda e, h2=h2, h4=h4: e.tensor_add(
                att_sb[:, h4, :, :], e0_sb[:, h2, :, :], d_sb[:, h2, :, :]),
               reads=[("d", h2), ("e0", h2)], writes=[("att", h4)])

        def attT_part(h):
            h4 = h % 4
            for it in range(4):
                bT = 3 if it % 2 == 0 else 2
                tv = (pbb[3][0:64, 0:128] if bT == 3
                      else pbb[2][0:64, 256:384])
                OP("pe", lambda e, h4=h4, it=it, bT=bT, tv=tv: e.transpose(
                    tv, att_sb[:, h4, it, :], id_sb[:]),
                   reads=[("att", h4), ("id",), PBK[bT]], writes=[PBK[bT]])
                v = tv.rearrange("p (rl cp ce) -> p rl cp ce", rl=8, cp=8, ce=2)
                OP("vec", lambda e, h=h, it=it, v=v, bT=bT: e.tensor_copy(
                    AF2[0:64, :, 32 * h + 8 * it:32 * h + 8 * it + 8],
                    v[:, :, :, 0].transpose([0, 2, 1])),
                   reads=[PBK[bT]], writes=[("AF2", h, it, 0)])
                OP("vec", lambda e, h=h, it=it, v=v, bT=bT: e.tensor_copy(
                    AF2[64:128, :, 32 * h + 8 * it:32 * h + 8 * it + 8],
                    v[:, :, :, 1].transpose([0, 2, 1])),
                   reads=[PBK[bT]], writes=[("AF2", h, it, 1)])

        scores_part(0)
        scores_part(1)
        for h in range(NH):
            if h + 2 < NH:
                scores_part(h + 2)
            umem_part(h)
            if h >= 2:
                attT_part(h - 2)
            if h == NH - 1 and t + 1 < nseg:
                emit_ln_prep(t + 1)
        attT_part(NH - 2)
        attT_part(NH - 1)

        af2_keys = [("AF2", h, it, ce) for h in range(NH) for it in range(4)
                    for ce in range(2)]

        for ot in range(4):
            op_ = ot % 2
            OP("sync", lambda e, t=t, ot=ot, op_=op_: e.dma_start(
                out=xsr[:, op_, :], in_=XdT[t, ot]),
               writes=[("xsl", op_)], dma=True)
            for dh in range(2):
                bk = 4 + dh
                for p8 in range(8):
                    OP("pe", lambda e, ot=ot, dh=dh, p8=p8, bk=bk: e.matmul(
                        pb[bk][:, :], AF2[:, p8, ot * 128:(ot + 1) * 128],
                        wo2_sb[:, p8, dh * 512:(dh + 1) * 512],
                        start=(p8 == 0), stop=(p8 == 7)),
                       reads=af2_keys + [("wos",), PBK[bk]], writes=[PBK[bk]])
                OP("vec", lambda e, dh=dh, bk=bk, op_=op_: e.tensor_copy(
                    ao_sb[:, op_, dh * 512:(dh + 1) * 512], pb[bk][:, :]),
                   reads=[PBK[bk]], writes=[("ao", op_)])
            if not dbg_raw_ao:
                ln_tile(ao_sb[:, op_, :], [("ao", op_)], op_,
                        ao_sb[:, op_, :], ("ao", op_))
                OP("vec", lambda e, op_=op_: e.tensor_add(
                    ao_sb[:, op_, :], ao_sb[:, op_, :], xsr[:, op_, :]),
                   reads=[("ao", op_), ("xsl", op_)], writes=[("ao", op_)])
            OP("sync", lambda e, ot=ot, op_=op_, t=t: e.dma_start(
                out=X1d[t * 512 + ot * 128:t * 512 + (ot + 1) * 128, :],
                in_=ao_sb[:, op_, :]),
               reads=[("ao", op_)], writes=[("x1d", t, ot)], dma=True)

    def emit_wcast(u):
        par = u % 2
        if u < 128:
            ex, ch = u // 16, u % 16
            src = Wu.ap().rearrange("E (kc p) n -> E p kc n", p=128)[
                ex, :, :, ch * 256:(ch + 1) * 256]
            dst = Wub.ap().rearrange("E (kc p) n -> E p kc n", p=128)[
                ex, :, :, ch * 256:(ch + 1) * 256]
        else:
            v = u - 128
            ex, ch = v // 8, v % 8
            src = Wd.ap().rearrange("E (hc p) n -> E p hc n", p=128)[
                ex, :, 2 * ch:2 * ch + 2, :]
            dst = Wdb.ap().rearrange("E (hc p) n -> E p hc n", p=128)[
                ex, :, 2 * ch:2 * ch + 2, :]
        OP("pool", lambda e, src=src, par=par: e.dma_start(
            out=wstg[:, par, :].rearrange("p (a b) -> p a b", a=8), in_=src),
           writes=[("wstg", par)], dma=True)
        OP("sync", lambda e, dst=dst, par=par: e.dma_start(
            out=dst, in_=wstg[:, par, :].rearrange("p (a b) -> p a b", a=8)),
           reads=[("wstg", par)], writes=[("wcast", u)], dma=True)

    if n_segs:
        emit_ln_prep(0)
        for t in range(n_segs):
            emit_seg(t, n_segs)
            if n_groups:
                for u in range(12 * t, min(192, 12 * (t + 1))):
                    emit_wcast(u)
        if n_groups:
            for u in range(12 * n_segs, 192):
                emit_wcast(u)

    S.barrier()

    OP("vec", lambda e: e.memset(cumA[0:8, 0:512], 0.0), writes=[("cumA",)])
    OP("vec", lambda e: e.memset(cumB[0:8, 0:512], 0.0), writes=[("cumB",)])
    OP("vec", lambda e: e.memset(dg_sb[64:128, :, 1, :], 0.0),
       writes=[("dginit",)])

    WuV = Wub.ap().rearrange("E (kc p) (j half n) -> E p kc j half n",
                             p=128, j=16, half=2, n=128)
    WdV = Wdb.ap().rearrange("E (hc p) n -> E p hc n", p=128)

    def emit_prologue(g):
        gp = g % 2
        for tt in range(NTT):
            tp = tt % 2
            r0 = g * TGRP + tt * 128
            OP("sync", lambda e, r0=r0, tp=tp: e.dma_start(
                out=x1t[:, tp, :], in_=X1d[r0:r0 + 128, :]),
               reads=[("x1d", r0 // 512, (r0 % 512) // 128)],
               writes=[("x1t", tp)], dma=True)
            ln_tile(x1t[:, tp, :], [("x1t", tp)], tp, yt_sb[:, tp, :], ("yt", tp))
            OP("vec", lambda e, tp=tp: e.bn_stats(
                stats[:, tp, 0:6], yt_sb[:, tp, 0:512]),
               reads=[("yt", tp)], writes=[("st_a", tp)])
            OP("vec", lambda e, tp=tp: e.bn_stats(
                stats[:, tp, 6:12], yt_sb[:, tp, 512:1024]),
               reads=[("yt", tp)], writes=[("st_b", tp)])
            OP("vec", lambda e, tp=tp: e.bn_aggr(
                mv[:, tp, :], stats[:, tp, :]),
               reads=[("st_a", tp), ("st_b", tp)], writes=[("mv", tp)])
            OP("vec", lambda e, tp=tp: e.tensor_scalar(
                out=rst[:, tp, 1:2], in0=mv[:, tp, 0:1],
                scalar1=mv[:, tp, 0:1], scalar2=None, op0=ALU.mult),
               reads=[("mv", tp)], writes=[("rt", tp)])
            OP("vec", lambda e, tp=tp: e.tensor_add(
                rst[:, tp, 1:2], rst[:, tp, 1:2], mv[:, tp, 1:2]),
               reads=[("rt", tp), ("mv", tp)], writes=[("rt", tp)])
            OP("act", lambda e, tp=tp: e.activation(
                out=rst[:, tp, 1:2], in_=rst[:, tp, 1:2],
                func=AF.Ln, scale=2.0 ** -40, bias=cst[:, 2:3]),
               reads=[("rt", tp), ("cst",)], writes=[("rt", tp)])
            OP("act", lambda e, tp=tp: e.activation(
                out=rst[:, tp, 0:1], in_=rst[:, tp, 1:2],
                func=AF.Exp, scale=-0.5, bias=cst[:, 4:5]),
               reads=[("rt", tp), ("cst",)], writes=[("rs", tp)])
            OP("vec", lambda e, tp=tp, tt=tt, gp=gp: e.tensor_scalar(
                out=xn_tok[:, gp, tt, :], in0=yt_sb[:, tp, :],
                scalar1=rst[:, tp, 0:1], scalar2=None, op0=ALU.mult),
               reads=[("yt", tp), ("rs", tp)], writes=[("xn", gp, tt)])
            for kc in range(8):
                bk = 6 + kc % 2
                OP("pe", lambda e, kc=kc, tt=tt, gp=gp, bk=bk: e.transpose(
                    pbb[bk][:, 0:128],
                    xn_tok[:, gp, tt, kc * 128:(kc + 1) * 128], id_sb[:]),
                   reads=[("xn", gp, tt), ("id",), PBK[bk]], writes=[PBK[bk]])
                OP("vec", lambda e, kc=kc, tp=tp, bk=bk: e.tensor_scalar(
                    out=xnTt[:, 0, kc, :],
                    in0=pbb[bk][:, 0:128], scalar1=sc_sb[:, kc:kc + 1],
                    scalar2=None, op0=ALU.mult),
                   reads=[PBK[bk], ("scs",)], writes=[("xnTt", kc)])
            for kc in range(8):
                OP("pe", lambda e, kc=kc, tp=tp: e.matmul(
                    pb[6][:, 256:264], xnTt[:, 0, kc, :],
                    wr_sb[:, kc, :], start=(kc == 0), stop=(kc == 7)),
                   reads=[("xnTt", kc), ("wrs",), PBK[6]],
                   writes=[PBK[6]])
            OP("vec", lambda e: e.tensor_copy(sc8[:, :], pb[6][:, 256:264]),
               reads=[PBK[6]], writes=[("sc8",)])
            OP("vec", lambda e: e.max(out=t8[:, :], in_=sc8[:, :]),
               reads=[("sc8",)], writes=[("t8",)])
            OP("act", lambda e: e.activation(
                out=r8[:, 1:2], in_=t8[:, 1:2], func=AF.Exp,
                bias=t8[:, 0:1], scale=-1.0),
               reads=[("t8",)], writes=[("r8b",)])
            OP("vec", lambda e: e.tensor_scalar(
                out=r8[:, 1:2], in0=r8[:, 1:2], scalar1=1.0, scalar2=None,
                op0=ALU.add),
               reads=[("r8b",)], writes=[("r8b",)])
            OP("vec", lambda e: e.reciprocal(r8[:, 2:3], r8[:, 1:2]),
               reads=[("r8b",)], writes=[("w2",)])
            OP("vec", lambda e: e.tensor_scalar(
                out=r8[:, 3:4], in0=r8[:, 2:3], scalar1=-1.0, scalar2=1.0,
                op0=ALU.mult, op1=ALU.add),
               reads=[("w2",)], writes=[("w1",)])
            OP("vec", lambda e: e.tensor_copy(r8[:, 4:5], t8[:, 1:2]),
               reads=[("t8",)], writes=[("m2c",)])
            OP("vec", lambda e, tt=tt, gp=gp: e.tensor_scalar(
                out=wm_sb[:, gp, tt, :], in0=sc8[:, :], scalar1=t8[:, 0:1],
                scalar2=r8[:, 3:4], op0=ALU.is_equal, op1=ALU.mult),
               reads=[("sc8",), ("t8",), ("w1",)], writes=[("wm", gp, tt)])
            OP("vec", lambda e: e.tensor_scalar(
                out=sc8[:, :], in0=sc8[:, :], scalar1=r8[:, 4:5],
                scalar2=r8[:, 2:3], op0=ALU.is_equal, op1=ALU.mult),
               reads=[("sc8",), ("m2c",), ("w2",)], writes=[("sc8",)])
            OP("vec", lambda e, tt=tt, gp=gp: e.tensor_add(
                wm_sb[:, gp, tt, :], wm_sb[:, gp, tt, :], sc8[:, :]),
               reads=[("wm", gp, tt), ("sc8",)], writes=[("wm", gp, tt)])

        for z in range(NTT // 4):
            OP("vec", lambda e, z=z, gp=gp: e.memset(
                accb[:, gp, 4 * z:4 * z + 4, :], 0.0),
               writes=[("accb", gp, z)])
        for sg in range(2):
            for t4 in range(4):
                tt = sg * 4 + t4
                OP("vec", lambda e, tt=tt, gp=gp: e.tensor_scalar(
                    out=m8_sb[:, :], in0=wm_sb[:, gp, tt, :], scalar1=0.0,
                    scalar2=None, op0=ALU.is_gt),
                   reads=[("wm", gp, tt)], writes=[("m8",)])
                tv = pbb[7][0:8, 896:1024]
                OP("pe", lambda e, tv=tv: e.transpose(tv, m8_sb[:, :],
                                                      id_sb[:]),
                   reads=[("m8",), ("id",), PBK[7]], writes=[PBK[7]])
                OP("vec", lambda e, t4=t4, tv=tv: e.tensor_copy(
                    cumA[0:8, 512 + t4 * 128:512 + (t4 + 1) * 128], tv),
                   reads=[PBK[7]], writes=[("cumA",)])
                OP("vec", lambda e, t4=t4, tv=tv: e.tensor_copy(
                    mT0[0:8, t4 * 128:(t4 + 1) * 128], tv),
                   reads=[PBK[7]], writes=[("mT0",)])
            src, dst = cumA, cumB
            sk_, dk_ = ("cumA",), ("cumB",)
            k = 1
            while k < 512:
                OP("vec", lambda e, src=src, dst=dst, k=k: e.tensor_add(
                    dst[0:8, 512:1024], src[0:8, 512:1024],
                    src[0:8, 512 - k:1024 - k]),
                   reads=[sk_], writes=[dk_])
                src, dst = dst, src
                sk_, dk_ = dk_, sk_
                k *= 2
            OP("vec", lambda e, src=src, sg=sg, gp=gp: e.tensor_tensor(
                out=slotTb[0:8, gp, sg * 512:(sg + 1) * 512],
                in0=src[0:8, 512:1024], in1=mT0[0:8, :], op=ALU.mult),
               reads=[sk_, ("mT0",)], writes=[("slotT", gp, sg)])
            OP("vec", lambda e, sg=sg, gp=gp: e.tensor_scalar(
                out=slotTb[0:8, gp, sg * 512:(sg + 1) * 512],
                in0=slotTb[0:8, gp, sg * 512:(sg + 1) * 512],
                scalar1=1.0, scalar2=None, op0=ALU.subtract),
               reads=[("slotT", gp, sg)], writes=[("slotT", gp, sg)])
            for t4 in range(4):
                tt = sg * 4 + t4
                tv2 = pbb[7][0:128, 896:904]
                OP("pe", lambda e, sg=sg, t4=t4, gp=gp, tv2=tv2: e.transpose(
                    tv2, slotTb[0:8, gp,
                                sg * 512 + t4 * 128:sg * 512 + (t4 + 1) * 128],
                    id_sb[0:8, 0:8]),
                   reads=[("slotT", gp, sg), ("id",), PBK[7]], writes=[PBK[7]])
                OP("vec", lambda e, tt=tt, gp=gp, tv2=tv2: e.tensor_copy(
                    slot_sb[:, gp, tt, :], tv2),
                   reads=[PBK[7]], writes=[("slot", gp, tt)])

    def gather_part(g, ex):
        gp = g % 2
        e2 = ex % 2
        for sg in range(2):
            for t4 in range(4):
                tt = sg * 4 + t4
                OP("vec", lambda e, t4=t4, tt=tt, gp=gp, ex=ex: e.tensor_scalar(
                    out=STe[:, t4, :], in0=iotaC[:, :],
                    scalar1=slot_sb[:, gp, tt, ex:ex + 1],
                    scalar2=None, op0=ALU.is_equal),
                   reads=[("iotaC",), ("slot", gp, tt)],
                   writes=[("STe", t4)])
            gb = 2 + sg
            for dt in range(8):
                for t4 in range(4):
                    tt = sg * 4 + t4
                    OP("pe", lambda e, dt=dt, t4=t4, tt=tt, gp=gp, gb=gb:
                       e.matmul(
                           pb[gb][:, 0:224],
                           xn_tok[:, gp, tt, dt * 128:(dt + 1) * 128],
                           STe[:, t4, :], start=(t4 == 0), stop=(t4 == 3)),
                       reads=[("xn", gp, tt), ("STe", t4), PBK[gb]],
                       writes=[PBK[gb]])
                OP("vec", lambda e, dt=dt, sg=sg, gb=gb, e2=e2: e.tensor_scalar(
                    out=xgT[:, e2, sg, dt, :], in0=pb[gb][:, 0:224],
                    scalar1=sc_sb[:, dt:dt + 1], scalar2=None, op0=ALU.mult),
                   reads=[PBK[gb], ("scs",)], writes=[("xgT", e2, sg, dt)])

    def compute_up(g, ex):
        e2 = ex % 2
        for c4 in range(4):
            OP("pool", lambda e, ex=ex, c4=c4: e.dma_start(
                out=wd_sb[:, 4 * c4:4 * c4 + 4, :],
                in_=WdV[ex, :, 4 * c4:4 * c4 + 4, :]),
               writes=[("wd", c4)], dma=True)
        for j in range(16):
            lb = j % 3
            OP("pool", lambda e, ex=ex, j=j, lb=lb: e.dma_start(
                out=wu_sb[:, lb, :, :],
                in_=WuV[ex, :, :, j, :, :].rearrange(
                    "p kc half n -> p kc (half n)")),
               writes=[("wu", lb)], dma=True)
            for half in range(2):
                for sg in range(2):
                    bk = half * 4 + sg
                    for kc in range(8):
                        OP("pe", lambda e, j=j, lb=lb, half=half, sg=sg,
                           kc=kc, bk=bk, e2=e2: e.matmul(
                               pb[bk][:, 0:224],
                               wu_sb[:, lb, kc, half * 128:half * 128 + 128],
                               xgT[:, e2, sg, kc, :],
                               start=(kc == 0), stop=(kc == 7)),
                           reads=[("wu", lb), ("xgT", e2, sg, kc), PBK[bk]],
                           writes=[PBK[bk]])
            for sg in range(2):
                OP("act", lambda e, sg=sg: e.activation(
                    out=sg_sb[:, sg, :], in_=pb[4 + sg][:, 0:224],
                    func=AF.Silu),
                   reads=[PBK[4 + sg]], writes=[("sg", sg)])
                OP("vec", lambda e, j=j, sg=sg: e.tensor_mul(
                    hg_sb[:, sg, j, :], pb[sg][:, 0:224], sg_sb[:, sg, :]),
                   reads=[PBK[sg], ("sg", sg)], writes=[("hg", sg, j)])

    def compute_rest(g, ex):
        gp = g % 2
        for sg in range(2):
            OP("pe", lambda e, ex=ex, sg=sg, gp=gp: e.matmul(
                pb[4 + sg][:, 0:512], e8_sb[0:8, ex, :],
                slotTb[0:8, gp, sg * 512:(sg + 1) * 512],
                start=True, stop=True),
               reads=[("e8",), ("slotT", gp, sg), PBK[4 + sg]],
               writes=[PBK[4 + sg]])
            for ck in range(2):
                OP("vec", lambda e, sg=sg, ck=ck: e.tensor_scalar(
                    out=SSb[:, sg, ck, :], in0=pb[4 + sg][:, 0:512],
                    scalar1=iota2[:, ck:ck + 1], scalar2=None,
                    op0=ALU.is_equal),
                   reads=[PBK[4 + sg], ("iota2",)], writes=[("SS", sg, ck)])
        for sg in range(2):
            hg_keys = [("hg", sg, j) for j in range(16)]
            for dh in range(2):
                for sl in range(2):
                    bk = 2 + sl
                    w = 128 if sl == 0 else 96
                    for hc in range(16):
                        OP("pe", lambda e, sg=sg, dh=dh, sl=sl, hc=hc, bk=bk,
                           w=w: e.matmul(
                               pb[bk][0:w, :],
                               hg_sb[:, sg, hc, sl * 128:sl * 128 + w],
                               wd_sb[:, hc, dh * 512:(dh + 1) * 512],
                               start=(hc == 0), stop=(hc == 15)),
                           reads=hg_keys + [("wd", hc // 4), PBK[bk]],
                           writes=[PBK[bk]])
                    OP("vec", lambda e, sg=sg, dh=dh, sl=sl, bk=bk, w=w:
                       e.tensor_copy(
                           dg_sb[0:w, sg, sl, dh * 512:(dh + 1) * 512],
                           pb[bk][0:w, :]),
                       reads=[PBK[bk]], writes=[("dg", sg, sl, dh)])
            dg_keys = [("dg", sg, sl, dh) for sl in range(2) for dh in range(2)]
            for t4 in range(4):
                tt = sg * 4 + t4
                for dh in range(2):
                    bk = 6 + dh
                    for ck in range(2):
                        OP("pe", lambda e, sg=sg, t4=t4, dh=dh, ck=ck, bk=bk:
                           e.matmul(
                               pb[bk][:, :],
                               SSb[:, sg, ck, t4 * 128:(t4 + 1) * 128],
                               dg_sb[:, sg, ck, dh * 512:(dh + 1) * 512],
                               start=(ck == 0), stop=(ck == 1)),
                           reads=[("SS", sg, ck), ("dginit",)] + dg_keys
                           + [PBK[bk]],
                           writes=[PBK[bk]])
                    OP("vec", lambda e, tt=tt, dh=dh, bk=bk, ex=ex, gp=gp:
                       e.scalar_tensor_tensor(
                           out=accb[:, gp, tt, dh * 512:(dh + 1) * 512],
                           in0=pb[bk][:, :], scalar=wm_sb[:, gp, tt, ex:ex + 1],
                           in1=accb[:, gp, tt, dh * 512:(dh + 1) * 512],
                           op0=ALU.mult, op1=ALU.add),
                       reads=[PBK[bk], ("wm", gp, tt), ("accb", gp, tt // 4)],
                       writes=[("accb", gp, tt // 4)])

    def emit_epilogue(g):
        gp = g % 2
        for tt in range(NTT):
            tp = 2 + tt % 2
            bi = tt % 2
            r0 = g * TGRP + tt * 128
            OP("sync", lambda e, r0=r0, bi=bi: e.dma_start(
                out=x1t[:, bi, :], in_=X1d[r0:r0 + 128, :]),
               writes=[("x1t", bi)], dma=True)
            ln_tile(x1t[:, bi, :], [("x1t", bi)], tp, yt_sb[:, bi, :],
                    ("yt", bi))
            OP("vec", lambda e, tt=tt, bi=bi, gp=gp: e.tensor_add(
                s_sb[:, 0, :], yt_sb[:, bi, :], accb[:, gp, tt, :]),
               reads=[("yt", bi), ("accb", gp, tt // 4)], writes=[("s",)])
            ln_tile(s_sb[:, 0, :], [("s",)], tp, s_sb[:, 0, :],
                    ("s",))
            OP("vec", lambda e, bi=bi: e.tensor_add(
                s_sb[:, 0, :], s_sb[:, 0, :], x1t[:, bi, :]),
               reads=[("s",), ("x1t", bi)], writes=[("s",)])
            OP("sync", lambda e, r0=r0: e.dma_start(
                out=Yd[r0:r0 + 128, :], in_=s_sb[:, 0, :]),
               reads=[("s",)], writes=[("yout", r0)], dma=True)

    if n_groups:
        emit_prologue(0)
        gather_part(0, 0)
        for g in range(n_groups):
            for ex in range(NEXP):
                compute_up(g, ex)
                if ex + 1 < NEXP:
                    gather_part(g, ex + 1)
                elif g + 1 < n_groups:
                    gather_part(g + 1, 0)
                compute_rest(g, ex)
                if ex == 0 and g >= 1:
                    emit_epilogue(g - 1)
                if ex == 2 and g + 1 < n_groups:
                    emit_prologue(g + 1)
        emit_epilogue(n_groups - 1)

    if dbg_taps:
        tapmap = {
            "T1": (T1, [("T1q", c) for c in range(16)] +
                   [("T1sq", rc) for rc in range(4)], BF16),
            "T2": (T2, [("T2k", c) for c in range(16)], BF16),
            "lnT": (lnT, [("lnT", kc, i) for kc in range(8) for i in range(4)],
                    BF16),
            "es": (es_sb, [("es", h2, jt) for h2 in range(2) for jt in range(4)], BF16),
            "vtk": (vtk, [("vtk", 0), ("vtk", 1), ("vtk1",)], BF16),
            "sktk": (sktk, [("sktk", 0), ("sktk", 1)], BF16),
            "AF2": (AF2, [("AF2", h, it, ce) for h in range(NH)
                          for it in range(4) for ce in range(2)], BF16),
            "Msb": (M_sb, [("Msb",)], F32),
            "G": (G_sb, [("G",)], F32),
            "e0": (e0_sb, [("e0", 0), ("e0", 1)], BF16),
            "e1": (e1_sb, [("e1", 0), ("e1", 1)], BF16),
            "att": (att_sb, [], BF16),
            "rc": (rc_sb, [("rc", 0), ("rc", 1)], F32),
            "ao": (ao_sb, [("ao", 0), ("ao", 1)], F32),
            "xnT": (xnT, [], BF16),
            "accb": (accb, [], BF16),
            "wm": (wm_sb, [], F32),
        }
        for name in dbg_taps:
            h, keys, dt = tapmap[name]
            shp = list(h.shape)
            dram = nc.dram_tensor("tap_" + name, shp, dt, kind="ExternalOutput")
            if dt == F32:
                OP("sync", lambda e, dram=dram, h=h: e.dma_start(
                    out=dram.ap(), in_=h.ap()), reads=keys, dma=True)
            else:
                OP("pool", lambda e, dram=dram, h=h: e.dma_start(
                    out=dram.ap(), in_=h.ap()), reads=keys, dma=True)

    semkeys = set()
    for o in S.ops:
        if o.dma:
            for ln in range(Sched.NLANES):
                semkeys.add((o.stream, True, ln))
        else:
            semkeys.add((o.stream, False))
    sems = {}
    for sk in semkeys:
        nm = (f"sem_{sk[0]}_d{sk[2]}" if len(sk) == 3 else f"sem_{sk[0]}_c")
        cm = nc.semaphore(nm)
        sems[sk] = cm.__enter__()
    S.finalize_and_emit(sems)
    nc._sched = S
    return nc


def _host_prep(inputs):
    x = np.ascontiguousarray(inputs["L0_attn_x"], dtype=np.float32)
    betas = np.asarray(inputs["L0_attn_betas"], dtype=np.float32).reshape(NH, DK)
    betab = np.ascontiguousarray(
        np.broadcast_to(betas.reshape(1, NH * DK), (128, NH * DK))).astype(np.float32)
    m0 = np.asarray(inputs["L0_attn_memory0"], dtype=np.float32).reshape(NH, DK, DK)
    z0 = np.asarray(inputs["L0_attn_z0"], dtype=np.float32).reshape(NH, DK)
    maug0 = np.empty((DK, NH, 65), np.float32)
    maug0[:, :, 0:64] = m0.transpose(1, 0, 2)
    maug0[:, :, 64] = z0.transpose(1, 0)
    nsc = np.ascontiguousarray(
        np.asarray(inputs["L0_ffn_norm_scale"], dtype=np.float32).reshape(8, 128).T)
    wrT = np.ascontiguousarray(
        np.asarray(inputs["L0_ffn_w_router"], dtype=np.float32).T)
    wut = np.asarray(inputs["L0_ffn_w_up"], dtype=np.float32).transpose(0, 2, 1)
    wut = wut.reshape(NEXP, D, 2, 16, 128).transpose(0, 1, 3, 2, 4)
    wut = np.ascontiguousarray(wut.reshape(NEXP, D, 2 * HID))
    wdt = np.ascontiguousarray(
        np.asarray(inputs["L0_ffn_w_down"], dtype=np.float32).transpose(0, 2, 1))
    common = {
        "wq": np.ascontiguousarray(inputs["L0_attn_Wq"], dtype=np.float32),
        "wk": np.ascontiguousarray(inputs["L0_attn_Wk"], dtype=np.float32),
        "wv": np.ascontiguousarray(inputs["L0_attn_Wv"], dtype=np.float32),
        "wo": np.ascontiguousarray(inputs["L0_attn_Wo"], dtype=np.float32),
        "betab": betab, "maug0": maug0, "nsc": nsc,
        "wr": wrT, "wu": wut, "wd": wdt,
    }
    return [dict(common, x=np.ascontiguousarray(x[c])) for c in range(8)]


def kernel(L0_attn_x, L0_attn_Wq, L0_attn_Wk, L0_attn_Wv, L0_attn_Wo,
           L0_attn_betas, L0_attn_memory0, L0_attn_z0, L0_ffn_norm_scale,
           L0_ffn_w_router, L0_ffn_w_up, L0_ffn_w_down):
    inputs = dict(
        L0_attn_x=L0_attn_x, L0_attn_Wq=L0_attn_Wq, L0_attn_Wk=L0_attn_Wk,
        L0_attn_Wv=L0_attn_Wv, L0_attn_Wo=L0_attn_Wo, L0_attn_betas=L0_attn_betas,
        L0_attn_memory0=L0_attn_memory0, L0_attn_z0=L0_attn_z0,
        L0_ffn_norm_scale=L0_ffn_norm_scale, L0_ffn_w_router=L0_ffn_w_router,
        L0_ffn_w_up=L0_ffn_w_up, L0_ffn_w_down=L0_ffn_w_down)
    in_maps = _host_prep(inputs)
    nc = build_program()
    res = run_bass_kernel_spmd(nc, in_maps, core_ids=list(range(8)))
    return np.stack([np.asarray(r["y"], dtype=np.float32) for r in res.results])


# version 09
# speedup vs baseline: 1.0703x; 1.0703x over previous
import numpy as np
import ml_dtypes
import concourse.bass as bass
import concourse.mybir as mybir
from concourse.bass_utils import run_bass_kernel_spmd

F32 = mybir.dt.float32
BF16 = mybir.dt.bfloat16
U8 = mybir.dt.uint8
AF = mybir.ActivationFunctionType
ALU = mybir.AluOpType

SEQ = 8192
D = 1024
NSEG = 16
LSEG = 512
NH = 16
DK = 64
NEXP = 8
HID = 2048
TGRP = 1024
NGRP = SEQ // TGRP
NTT = TGRP // 128
NTC = TGRP // 512

EXP_SHIFT = -30.0


class _Op:
    __slots__ = ("stream", "fn", "deps", "dma", "inc", "tag")

    def __init__(self, stream, fn, deps, dma, tag):
        self.stream = stream
        self.fn = fn
        self.deps = deps
        self.dma = dma
        self.inc = False
        self.tag = tag


class Sched:
    STREAMS = ("pe", "vec", "act", "pool", "sync")

    def __init__(self, nc):
        self.nc = nc
        self.ops = []
        self.writer = {}
        self.readers = {}
        self.barrier_deps = ()

    def op(self, stream, fn, reads=(), writes=(), dma=False, tag=None):
        idx = len(self.ops)
        deps = set(self.barrier_deps)
        for k in reads:
            w = self.writer.get(k)
            if w is not None:
                deps.add(w)
        for k in writes:
            w = self.writer.get(k)
            if w is not None:
                deps.add(w)
            deps.update(self.readers.get(k, ()))
        deps.discard(idx)
        best = {}
        keep = []
        for d in deps:
            o = self.ops[d]
            if o.stream == "pe" and not o.dma and stream == "pe" and not dma:
                continue
            if o.dma:
                keep.append(d)
                continue
            key = o.stream
            if best.get(key, -1) < d:
                best[key] = d
        self.ops.append(_Op(stream, fn, tuple(sorted(best.values()) + keep),
                            dma, tag))
        for k in reads:
            self.readers.setdefault(k, []).append(idx)
        for k in writes:
            self.writer[k] = idx
            self.readers[k] = [idx]
        return idx

    def barrier(self):
        last = {}
        lastd = {}
        for i, o in enumerate(self.ops):
            if o.dma:
                lastd.setdefault(o.stream, []).append(i)
            else:
                last[(o.stream,)] = i
        deps = list(last.values())
        for q, lst in lastd.items():
            deps.extend(lst[-self.NLANES:])
        self.barrier_deps = tuple(deps)

    NLANES = 10

    def finalize_and_emit(self, sems):
        for o in self.ops:
            if o.dma:
                o.inc = True
            for d in o.deps:
                self.ops[d].inc = True
        pos = {}
        semkey_of = {}
        cnt = {}
        qidx = {}
        for i, o in enumerate(self.ops):
            if not o.inc:
                continue
            if o.dma:
                qi = qidx.get(o.stream, 0)
                qidx[o.stream] = qi + 1
                k = (o.stream, True, qi % self.NLANES)
                pos[i] = qi // self.NLANES + 1
            else:
                k = (o.stream, False)
                cnt[k] = cnt.get(k, 0) + 1
                pos[i] = cnt[k]
            semkey_of[i] = k
        self.pos = pos
        self.semkey_of = semkey_of
        final = dict(cnt)
        for i, o in enumerate(self.ops):
            if o.inc and o.dma:
                k = semkey_of[i]
                final[k] = max(final.get(k, 0), pos[i])
        self.final_counts = final

        per_stream = {s: [] for s in self.STREAMS}
        for i, o in enumerate(self.ops):
            per_stream[o.stream].append(i)

        def emit_stream(stream, eng):
            waited = {}

            def wait_for(dep):
                do = self.ops[dep]
                semkey = self.semkey_of[dep]
                val = self.pos[dep] * (16 if do.dma else 1)
                if waited.get(semkey, 0) >= val:
                    return
                eng.wait_ge(sems[semkey], val)
                waited[semkey] = val

            for i in per_stream[stream]:
                o = self.ops[i]
                for d in o.deps:
                    wait_for(d)
                if o.dma:
                    semkey = self.semkey_of[i]
                    gate = (self.pos[i] - 1) * 16
                    if gate > 0 and waited.get(semkey, 0) < gate:
                        eng.wait_ge(sems[semkey], gate)
                        waited[semkey] = gate
                inst = o.fn(eng)
                if o.inc:
                    semkey = self.semkey_of[i]
                    inst.then_inc(sems[semkey], 16 if o.dma else 1)
            if stream == "sync":
                for semkey, c in self.final_counts.items():
                    if len(semkey) == 3:
                        eng.wait_ge(sems[semkey], 16 * c)

        nc = self.nc
        with nc.Block() as block:
            @block.sync
            def _(eng):
                emit_stream("sync", eng)

            @block.tensor
            def _(eng):
                emit_stream("pe", eng)

            @block.vector
            def _(eng):
                emit_stream("vec", eng)

            @block.scalar
            def _(eng):
                emit_stream("act", eng)

            @block.gpsimd
            def _(eng):
                emit_stream("pool", eng)


def build_program(n_segs=NSEG, n_groups=NGRP, dbg_x1_out=False, dbg_taps=None, dbg_raw_ao=False):
    nc = bass.Bass("TRN2", target_bir_lowering=False)

    Xd = nc.dram_tensor("x", [SEQ, D], F32, kind="ExternalInput")
    Wq = nc.dram_tensor("wq", [D, D], F32, kind="ExternalInput")
    Wk = nc.dram_tensor("wk", [D, D], F32, kind="ExternalInput")
    Wv = nc.dram_tensor("wv", [D, D], F32, kind="ExternalInput")
    Wo = nc.dram_tensor("wo", [D, D], F32, kind="ExternalInput")
    Betab = nc.dram_tensor("betab", [128, NH * DK], F32, kind="ExternalInput")
    Maug0 = nc.dram_tensor("maug0", [DK, NH, 65], F32, kind="ExternalInput")
    Scale = nc.dram_tensor("nsc", [128, 8], F32, kind="ExternalInput")
    Wr = nc.dram_tensor("wr", [D, NEXP], F32, kind="ExternalInput")
    Wu = nc.dram_tensor("wu", [NEXP, D, 2 * HID], F32, kind="ExternalInput")
    Wd = nc.dram_tensor("wd", [NEXP, HID, D], F32, kind="ExternalInput")
    Yd = nc.dram_tensor("y", [SEQ, D], F32, kind="ExternalOutput")
    x1_kind = "ExternalOutput" if dbg_x1_out else "Internal"
    X1d = nc.dram_tensor("x1", [SEQ, D], F32, kind=x1_kind)
    Vpd = nc.dram_tensor("vpd", [2, LSEG, D], BF16, kind="Internal")
    Skd = nc.dram_tensor("skd", [2, LSEG, D], BF16, kind="Internal")

    idn = nc.inline_tensor(np.eye(128, dtype=ml_dtypes.bfloat16), "idn")
    iotaC_np = np.broadcast_to(np.arange(224, dtype=np.float32)[None, :],
                               (128, 224)).copy()
    iotaCh = nc.inline_tensor(iotaC_np, "iotac")
    iota2_np = np.stack([np.arange(128, dtype=np.float32),
                         np.arange(128, dtype=np.float32) + 128.0], 1).copy()
    iota2h = nc.inline_tensor(iota2_np, "iota2")
    e8_np = np.zeros((8, 8, 128), np.float32)
    for _k in range(8):
        e8_np[_k, _k, :] = 1.0
    e8h = nc.inline_tensor(e8_np.astype(ml_dtypes.bfloat16), "e8b")
    tril = np.triu(np.ones((128, 128), np.float32)).astype(ml_dtypes.bfloat16)
    trilh = nc.inline_tensor(tril, "trilc")

    sbuf = lambda n, s, d=F32: nc.alloc_sbuf_tensor(n, s, d)
    id_sb = sbuf("id_sb", [128, 128], BF16)
    mk_sb = sbuf("mk_sb", [128, 128], BF16)
    cst = sbuf("cst", [128, 8], F32)
    G_sb = sbuf("G_sb", [128, NH, DK], BF16)
    betat = sbuf("betat", [128, NH * DK], F32)
    sc_sb = sbuf("sc_sb", [128, 8], F32)
    M_sb = sbuf("M_sb", [DK, NH, 65], F32)
    M2_sb = sbuf("M2_sb", [128, NH, 65], BF16)
    stats = sbuf("stats", [128, 4, 12], F32)
    mv = sbuf("mv", [128, 4, 2], F32)
    rst = sbuf("rst", [128, 4, 2], F32)
    wr_sb = sbuf("wr_sb", [128, 8, NEXP], BF16)
    att_sb = sbuf("att_sb", [128, 4, 4, DK], BF16)
    e0_sb = sbuf("e0_sb", [128, 2, 4, DK], BF16)
    e1_sb = sbuf("e1_sb", [128, 2, 4, DK], BF16)
    d_sb = sbuf("d_sb", [128, 2, 4, DK], BF16)
    rc_sb = sbuf("rc_sb", [128, 2, 2, 4], F32)
    sc8 = sbuf("sc8", [128, 8], F32)
    iotaC = sbuf("iotaCs", [128, 224], F32)
    iota2 = sbuf("iota2s", [128, 2], F32)
    e8_sb = sbuf("e8_sb", [8, 8, 128], BF16)
    t8 = sbuf("t8", [128, 8], F32)
    r8 = sbuf("r8", [128, 8], F32)

    ARENA = 180 * 1024
    AR = nc.alloc_sbuf_tensor("arena", [128, ARENA], U8)
    arena_base = nc.lookup_mloc(AR).addr

    def mkplacer():
        state = {"off": 0}

        def place(name, shape, dtype):
            per = int(np.prod(shape[1:])) * mybir.dt.size(dtype)
            off = (state["off"] + 31) & ~31
            state["off"] = off + per
            assert state["off"] <= ARENA, (name, state["off"])
            h = nc.virtual_sbuf_tensor(name, list(shape), dtype)
            nc._tensor(h.name, list(h.shape), dtype, type="SB")
            mloc = nc.lookup_mloc(h)
            mloc.allocated = True
            mloc.addr = arena_base + off
            return h
        return place

    pa = mkplacer()
    wq_sb = pa("wq_sb", [128, 8, D], BF16)
    wk_sb = pa("wk_sb", [128, 8, D], BF16)
    wv_sb = pa("wv_sb", [128, 8, D], BF16)
    wo2_sb = pa("wo2_sb", [128, 8, D], BF16)
    T1 = pa("T1", [128, LSEG, 16], BF16)
    T2 = pa("T2", [128, LSEG, 16], BF16)
    sqscr = pa("sqscr", [DK, 128, 16], BF16)
    lnT = pa("lnT", [128, 8, LSEG], BF16)
    ln_sb = pa("ln_sb", [128, 2, D], BF16)
    xsl = pa("xsl", [128, 2, D], F32)
    xsr = xsl
    es_sb = pa("es_sb", [128, 2, 4, LSEG], BF16)
    vtk = pa("vtk", [128, 2, 4, 4, 65], BF16)
    sktk = pa("sktk", [128, 2, 4, 4, DK], BF16)
    ktmp = pa("ktmp", [128, 2, D], BF16)
    sktmp = pa("sktmp", [128, 2, D], BF16)
    vp_sb = pa("vp_sb", [128, 2, D], BF16)
    AF2 = pa("AF2", [128, 8, LSEG], BF16)
    ao_sb = pa("ao_sb", [128, 2, D], F32)

    pb_ = mkplacer()
    xn_tok = pb_("xn_tok", [128, 2, 8, D], BF16)
    xnTt = pb_("xnTt", [128, 1, 8, 128], BF16)
    xgT = pb_("xgT", [128, 2, 8, 224], BF16)
    hg_sb = pb_("hg_sb", [128, 2, 16, 224], BF16)
    dg_sb = pb_("dg_sb", [128, 2, 2, D], BF16)
    STe = pb_("STe", [128, 4, 224], BF16)
    SSb = pb_("SSb", [128, 2, 2, 512], BF16)
    slot_sb = pb_("slot_sb", [128, 2, 8, 8], F32)
    cumA = pb_("cumA", [8, 1024], BF16)
    cumB = pb_("cumB", [8, 1024], BF16)
    mT0 = pb_("mT0", [8, 512], BF16)
    slotTb = pb_("slotTb", [8, 2, 1024], BF16)
    m8_sb = pb_("m8_sb", [128, 8], BF16)
    accb = pb_("accb", [128, 2, NTT, D], BF16)
    wu_sb = pb_("wu_sb", [128, 3, 8, 256], BF16)
    wd_sb = pb_("wd_sb", [128, 16, D], BF16)
    x1t = pb_("x1t", [128, 2, D], F32)
    yt_sb = pb_("yt_sb", [128, 2, D], F32)
    xn_sb = pb_("xn_sb", [128, 2, D], BF16)
    sg_sb = pb_("sg_sb", [128, 2, 224], BF16)
    s_sb = pb_("s_sb", [128, 1, D], F32)
    wm_sb = pb_("wm_sb", [128, 2, NTT, NEXP], F32)

    pb = [nc.alloc_psum_tensor(f"pb{i}", [128, 512], F32) for i in range(8)]
    pbb = [p.ap().bitcast(BF16) for p in pb]

    S = Sched(nc)
    OP = S.op
    PBK = [(f"pb{i}",) for i in range(8)]

    OP("sync", lambda e: e.dma_start(out=id_sb[:], in_=idn[:]),
       writes=[("id",)], dma=True)
    OP("sync", lambda e: e.dma_start(out=mk_sb[:], in_=trilh[:]),
       writes=[("mk",)], dma=True)
    OP("sync", lambda e: e.dma_start(out=betat[:], in_=Betab[:]),
       writes=[("betat",)], dma=True)
    OP("sync", lambda e: e.dma_start(out=M_sb[:, :, :], in_=Maug0[:]),
       writes=[("Msb",)], dma=True)
    OP("sync", lambda e: e.dma_start(out=sc_sb[:], in_=Scale[:]),
       writes=[("scs",)], dma=True)
    OP("sync", lambda e: e.dma_start(out=iotaC[:], in_=iotaCh[:]),
       writes=[("iotaC",)], dma=True)
    OP("sync", lambda e: e.dma_start(out=iota2[:], in_=iota2h[:]),
       writes=[("iota2",)], dma=True)
    OP("sync", lambda e: e.dma_start(out=e8_sb[:, :, :], in_=e8h[:]),
       writes=[("e8",)], dma=True)
    OP("pool", lambda e: e.dma_start(
        out=wq_sb[:, :, :], in_=Wq.ap().rearrange("(kc p) n -> p kc n", p=128)),
        writes=[("wqs",)], dma=True)
    OP("pool", lambda e: e.dma_start(
        out=wk_sb[:, :, :], in_=Wk.ap().rearrange("(kc p) n -> p kc n", p=128)),
        writes=[("wks",)], dma=True)
    OP("pool", lambda e: e.dma_start(
        out=wv_sb[:, :, :], in_=Wv.ap().rearrange("(kc p) n -> p kc n", p=128)),
        writes=[("wvs",)], dma=True)
    OP("pool", lambda e: e.dma_start(
        out=wo2_sb[:, :, :],
        in_=Wo.ap().rearrange("(p8 ce dp) n -> (ce dp) p8 n", p8=8, ce=2, dp=64)),
        writes=[("wos",)], dma=True)
    OP("pool", lambda e: e.dma_start(
        out=wr_sb[:, :, :], in_=Wr.ap().rearrange("(kc p) n -> p kc n", p=128)),
        writes=[("wrs",)], dma=True)

    OP("vec", lambda e: e.memset(cst[:, 0:1], EXP_SHIFT), writes=[("cst",)])
    OP("vec", lambda e: e.memset(cst[:, 1:2], 1e-5 * 2.0 ** -44), writes=[("cst",)])
    OP("vec", lambda e: e.memset(cst[:, 2:3], 1e-6 * 2.0 ** -40), writes=[("cst",)])
    OP("vec", lambda e: e.memset(cst[:, 3:4], -22.0 * 0.6931471805599453), writes=[("cst",)])
    OP("vec", lambda e: e.memset(cst[:, 4:5], -20.0 * 0.6931471805599453), writes=[("cst",)])
    OP("vec", lambda e: e.memset(vtk[:, :, :, :, 64:65], 1.0), writes=[("vtk1",)])
    OP("act", lambda e: e.activation(out=betat[:], in_=betat[:], func=AF.Exp,
                                     scale=-1.0),
       reads=[("betat",)], writes=[("betat",)])
    OP("vec", lambda e: e.tensor_scalar(out=betat[:], in0=betat[:], scalar1=1.0,
                                        scalar2=None, op0=ALU.add),
       reads=[("betat",)], writes=[("betat",)])
    def _grecip(e):
        with nc.allow_low_precision(reason="gate sigmoid output in bf16"):
            return e.reciprocal(G_sb.ap().rearrange("p a b -> p (a b)"),
                                betat[:])
    OP("vec", _grecip, reads=[("betat",)], writes=[("G",)])

    XdT = Xd.ap().rearrange("(t tt p) d -> t tt p d", t=NSEG, tt=4, p=128)
    VpdV = Vpd.ap().rearrange("b (h jt rr) (c dp) -> b (rr c) h jt dp",
                              h=NH, jt=4, rr=8, c=16, dp=64)
    SkdV = Skd.ap().rearrange("b (h jt rr) (c dp) -> b (rr c) h jt dp",
                              h=NH, jt=4, rr=8, c=16, dp=64)

    def ln_tile(xap, xkeys, par, out_ap, outkey):
        OP("vec", lambda e: e.bn_stats(stats[:, par, 0:6], xap[:, 0:512]),
           reads=list(xkeys), writes=[("st_a", par)])
        OP("vec", lambda e: e.bn_stats(stats[:, par, 6:12], xap[:, 512:1024]),
           reads=list(xkeys), writes=[("st_b", par)])
        OP("vec", lambda e: e.bn_aggr(mv[:, par, :], stats[:, par, :]),
           reads=[("st_a", par), ("st_b", par)], writes=[("mv", par)])
        OP("act", lambda e: e.activation(out=rst[:, par, 1:2], in_=mv[:, par, 1:2],
                                         func=AF.Ln, scale=2.0 ** -44,
                                         bias=cst[:, 1:2]),
           reads=[("mv", par), ("cst",)], writes=[("rt", par)])
        OP("act", lambda e: e.activation(out=rst[:, par, 0:1], in_=rst[:, par, 1:2],
                                         func=AF.Exp, scale=-0.5, bias=cst[:, 3:4]),
           reads=[("rt", par), ("cst",)], writes=[("rs", par)])
        OP("vec", lambda e: e.tensor_scalar(out=out_ap, in0=xap,
                                            scalar1=mv[:, par, 0:1],
                                            scalar2=rst[:, par, 0:1],
                                            op0=ALU.subtract, op1=ALU.mult),
           reads=list(xkeys) + [("mv", par), ("rs", par)], writes=[outkey])

    def emit_seg(t):
        px = t % 2
        OP("vec", lambda e: e.tensor_copy(M2_sb[64:128, :, :], M_sb[:, :, :]),
           reads=[("Msb",)], writes=[("M2",)])

        for i in range(4):
            ip = i % 2
            OP("sync", lambda e, t=t, i=i, ip=ip: e.dma_start(
                out=xsl[:, ip, :], in_=XdT[t, i]),
               writes=[("xsl", ip)], dma=True)
            ln_tile(xsl[:, ip, :], [("xsl", ip)], ip, ln_sb[:, ip, :], ("ln", ip))
            for kc in range(8):
                bk = 4 + (kc % 4)
                OP("pe", lambda e, kc=kc, ip=ip, bk=bk: e.transpose(
                    pbb[bk][:, 0:128], ln_sb[:, ip, kc * 128:(kc + 1) * 128],
                    id_sb[:]),
                   reads=[("ln", ip), ("id",), PBK[bk]], writes=[PBK[bk]])
                OP("vec", lambda e, i=i, kc=kc, bk=bk: e.tensor_copy(
                    lnT[:, kc, i * 128:(i + 1) * 128], pbb[bk][:, 0:128]),
                   reads=[PBK[bk]], writes=[("lnT", kc, i)])

        lnT_keys = [("lnT", kc, i) for kc in range(8) for i in range(4)]

        for tt in range(4):
            tp = tt % 2
            for dh in range(2):
                bk = dh
                for kc in range(8):
                    OP("pe", lambda e, tt=tt, dh=dh, kc=kc, bk=bk: e.matmul(
                        pb[bk][:, :], lnT[:, kc, tt * 128:(tt + 1) * 128],
                        wv_sb[:, kc, dh * 512:(dh + 1) * 512],
                        start=(kc == 0), stop=(kc == 7)),
                       reads=[("lnT", kc, tt), ("wvs",), PBK[bk]],
                       writes=[PBK[bk]])
                OP("vec", lambda e, dh=dh, bk=bk, tp=tp: e.tensor_copy(
                    vp_sb[:, tp, dh * 512:(dh + 1) * 512], pb[bk][:, :]),
                   reads=[PBK[bk]], writes=[("vp", tp, dh)])
            OP("sync", lambda e, tt=tt, tp=tp, px=px: e.dma_start(
                out=Vpd[px, tt * 128:(tt + 1) * 128, :], in_=vp_sb[:, tp, :]),
               reads=[("vp", tp, 0), ("vp", tp, 1)], writes=[("vpd", px, tt)],
               dma=True)
        vpd_keys = [("vpd", px, tt) for tt in range(4)]

        def load_vtk(hq):
            hp = hq % 2
            OP("sync", lambda e, hq=hq, hp=hp, px=px: e.dma_start(
                out=vtk[:, hp, :, :, 0:64],
                in_=VpdV[px, :, 4 * hq:4 * hq + 4, :, :]),
               reads=vpd_keys, writes=[("vtk", hp)], dma=True)

        for tt in range(4):
            tp = tt % 2
            for dh in range(2):
                bk = 2 + dh
                for kc in range(8):
                    OP("pe", lambda e, tt=tt, dh=dh, kc=kc, bk=bk: e.matmul(
                        pb[bk][:, :], lnT[:, kc, tt * 128:(tt + 1) * 128],
                        wk_sb[:, kc, dh * 512:(dh + 1) * 512],
                        start=(kc == 0), stop=(kc == 7)),
                       reads=[("lnT", kc, tt), ("wks",), PBK[bk]],
                       writes=[PBK[bk]])
                OP("vec", lambda e, dh=dh, bk=bk, tp=tp: e.tensor_copy(
                    ktmp[:, tp, dh * 512:(dh + 1) * 512], pb[bk][:, :]),
                   reads=[PBK[bk]], writes=[("ktmp", tp)])
            OP("vec", lambda e, tp=tp: e.tensor_scalar_min(
                sktmp[:, tp, :], ktmp[:, tp, :], 0.0),
               reads=[("ktmp", tp)], writes=[("sktmp", tp)])
            OP("act", lambda e, tp=tp: e.activation(
                out=sktmp[:, tp, :], in_=sktmp[:, tp, :], func=AF.Exp),
               reads=[("sktmp", tp)], writes=[("sktmp", tp)])
            OP("vec", lambda e, tp=tp: e.scalar_tensor_tensor(
                out=sktmp[:, tp, :], in0=ktmp[:, tp, :], scalar=0.0,
                in1=sktmp[:, tp, :], op0=ALU.max, op1=ALU.add),
               reads=[("ktmp", tp), ("sktmp", tp)], writes=[("sktmp", tp)])
            OP("sync", lambda e, tt=tt, tp=tp, px=px: e.dma_start(
                out=Skd[px, tt * 128:(tt + 1) * 128, :], in_=sktmp[:, tp, :]),
               reads=[("sktmp", tp)], writes=[("skd", px, tt)], dma=True)
        skd_keys = [("skd", px, tt) for tt in range(4)]

        def load_sktk(hq):
            hp = hq % 2
            OP("sync", lambda e, hq=hq, hp=hp, px=px: e.dma_start(
                out=sktk[:, hp, :, :, :],
                in_=SkdV[px, :, 4 * hq:4 * hq + 4, :, :]),
               reads=skd_keys, writes=[("sktk", hp)], dma=True)
        load_vtk(0)
        load_sktk(0)
        load_vtk(1)
        load_sktk(1)

        for (wsb, wkey, dst, dkey, bk0) in (
                (wq_sb, ("wqs",), T1, "T1q", 0), (wk_sb, ("wks",), T2, "T2k", 2)):
            for p8 in range(8):
                bk = bk0 + p8 % 2
                for kc in range(8):
                    OP("pe", lambda e, p8=p8, kc=kc, bk=bk, wsb=wsb, wkey=wkey:
                       e.matmul(pb[bk][:, :], wsb[:, kc, p8 * 128:(p8 + 1) * 128],
                                lnT[:, kc, :], start=(kc == 0), stop=(kc == 7)),
                       reads=lnT_keys + [wkey, PBK[bk]], writes=[PBK[bk]])
                OP("vec", lambda e, p8=p8, bk=bk, dst=dst: e.tensor_copy(
                    dst[0:64, :, 2 * p8:2 * p8 + 1],
                    pb[bk][0:64, :].unsqueeze(2)),
                   reads=[PBK[bk]], writes=[(dkey, 2 * p8)])
                OP("vec", lambda e, p8=p8, bk=bk, dst=dst: e.tensor_copy(
                    dst[0:64, :, 2 * p8 + 1:2 * p8 + 2],
                    pb[bk][64:128, :].unsqueeze(2)),
                   reads=[PBK[bk]], writes=[(dkey, 2 * p8 + 1)])
        t1q_keys = [("T1q", c) for c in range(16)]
        t2k_keys = [("T2k", c) for c in range(16)]

        for rc in range(4):
            sl = slice(rc * 128, (rc + 1) * 128)
            OP("vec", lambda e, sl=sl: e.tensor_scalar_min(
                sqscr[:, :, :], T1[0:64, sl, :], 0.0),
               reads=t1q_keys, writes=[("sqs",)])
            OP("act", lambda e: e.activation(
                out=sqscr[:, :, :], in_=sqscr[:, :, :], func=AF.Exp),
               reads=[("sqs",)], writes=[("sqs",)])
            OP("vec", lambda e, sl=sl: e.scalar_tensor_tensor(
                out=sqscr[:, :, :], in0=T1[0:64, sl, :], scalar=0.0,
                in1=sqscr[:, :, :], op0=ALU.max, op1=ALU.add),
               reads=t1q_keys + [("sqs",)], writes=[("sqs",)])
            OP("vec", lambda e, sl=sl, rc=rc: e.tensor_copy(
                T1[64:128, sl, :], sqscr[:, :, :]),
               reads=[("sqs",)], writes=[("T1sq", rc)])

        def scores_part(h):
            hq = h // 4
            h2 = h % 2
            if h % 4 == 0 and hq >= 2:
                load_vtk(hq)
                load_sktk(hq)
            for jt in range(4):
                bk = jt % 2
                n = 512 - 128 * jt
                OP("pe", lambda e, h=h, jt=jt, bk=bk, n=n: e.matmul(
                    pb[bk][:, 0:n],
                    T2[0:64, 32 * h + 8 * jt:32 * h + 8 * jt + 8, :]
                    .rearrange("p a b -> p (a b)"),
                    T1[0:64, 32 * h + 8 * jt:32 * h + 32, :]
                    .rearrange("p a b -> p (a b)"),
                    start=True, stop=True),
                   reads=t1q_keys + t2k_keys + [PBK[bk]], writes=[PBK[bk]])
                OP("act", lambda e, jt=jt, bk=bk, n=n, h2=h2: e.activation(
                    out=es_sb[:, h2, jt, 128 * jt:512], in_=pb[bk][:, 0:n],
                    func=AF.Exp, bias=cst[:, 0:1], scale=0.125),
                   reads=[PBK[bk], ("cst",)], writes=[("es", h2, jt)])
                OP("vec", lambda e, jt=jt, h2=h2: e.tensor_tensor(
                    out=es_sb[:, h2, jt, 128 * jt:128 * jt + 128],
                    in0=es_sb[:, h2, jt, 128 * jt:128 * jt + 128],
                    in1=mk_sb[:], op=ALU.mult),
                   reads=[("es", h2, jt), ("mk",)], writes=[("es", h2, jt)])

        def umem_part(h):
            hq, hi, hp = h // 4, h % 4, (h // 4) % 2
            h2 = h % 2
            bU = 6 if h2 == 0 else 4
            bM = 7 if h2 == 0 else 5
            for it in range(4):
                for jt in range(it + 1):
                    OP("pe", lambda e, it=it, jt=jt, hp=hp, hi=hi, h2=h2,
                       bU=bU: e.matmul(
                        pb[bU][:, it * 65:(it + 1) * 65],
                        es_sb[:, h2, jt, 128 * it:128 * it + 128],
                        vtk[:, hp, hi, jt, :],
                        start=(jt == 0), stop=(jt == it)),
                       reads=[("es", h2, jt), ("vtk", hp), ("vtk1",), PBK[bU]],
                       writes=[PBK[bU]])
            for it in range(4):
                OP("pe", lambda e, h=h, it=it, bM=bM: e.matmul(
                    pb[bM][:, it * 65:(it + 1) * 65],
                    T1[64:128, 32 * h + 8 * it:32 * h + 8 * it + 8, :]
                    .rearrange("p a b -> p (a b)"),
                    M2_sb[64:128, h, :], start=True, stop=True),
                   reads=[("T1sq", (32 * h) // 128), ("M2",), PBK[bM]],
                   writes=[PBK[bM]])
            for jt in range(4):
                OP("pe", lambda e, jt=jt, hp=hp, hi=hi: e.matmul(
                    pb[2][0:64, 0:65],
                    sktk[:, hp, hi, jt, :], vtk[:, hp, hi, jt, :],
                    start=(jt == 0), stop=(jt == 3)),
                   reads=[("sktk", hp), ("vtk", hp), ("vtk1",), PBK[2]],
                   writes=[PBK[2]])
            OP("vec", lambda e, h=h: e.tensor_add(
                M_sb[:, h, :], M_sb[:, h, :], pb[2][0:64, 0:65]),
               reads=[PBK[2], ("Msb",)], writes=[("Msb",)])
            pU = pb[bU][:, 0:260].rearrange("p (a b) -> p a b", b=65)
            pM = pb[bM][:, 0:260].rearrange("p (a b) -> p a b", b=65)
            OP("vec", lambda e, pU=pU, h2=h2: e.reciprocal(
                rc_sb[:, h2, 0, :], pU[:, :, 64]),
               reads=[PBK[bU]], writes=[("rc", h2)])
            OP("vec", lambda e, pM=pM, h2=h2: e.reciprocal(
                rc_sb[:, h2, 1, :], pM[:, :, 64]),
               reads=[PBK[bM]], writes=[("rc", h2)])
            OP("vec", lambda e, pU=pU, h2=h2: e.tensor_mul(
                e0_sb[:, h2, :, :], pU[:, :, 0:64],
                rc_sb[:, h2, 0, :].unsqueeze(2).broadcast_to([128, 4, 64])),
               reads=[PBK[bU], ("rc", h2)], writes=[("e0", h2)])
            OP("vec", lambda e, pM=pM, h2=h2: e.tensor_mul(
                e1_sb[:, h2, :, :], pM[:, :, 0:64],
                rc_sb[:, h2, 1, :].unsqueeze(2).broadcast_to([128, 4, 64])),
               reads=[PBK[bM], ("rc", h2)], writes=[("e1", h2)])
            OP("vec", lambda e, h2=h2: e.tensor_sub(
                d_sb[:, h2, :, :], e1_sb[:, h2, :, :], e0_sb[:, h2, :, :]),
               reads=[("e0", h2), ("e1", h2)], writes=[("d", h2)])
            OP("vec", lambda e, h2=h2, h=h: e.tensor_mul(
                d_sb[:, h2, :, :], d_sb[:, h2, :, :],
                G_sb[:, h, :].unsqueeze(1).broadcast_to([128, 4, 64])),
               reads=[("d", h2), ("G",)], writes=[("d", h2)])
            h4 = h % 4
            OP("vec", lambda e, h2=h2, h4=h4: e.tensor_add(
                att_sb[:, h4, :, :], e0_sb[:, h2, :, :], d_sb[:, h2, :, :]),
               reads=[("d", h2), ("e0", h2)], writes=[("att", h4)])

        def attT_part(h):
            h4 = h % 4
            for it in range(4):
                bT = 3 if it % 2 == 0 else 2
                tv = (pbb[3][0:64, 0:128] if bT == 3
                      else pbb[2][0:64, 256:384])
                OP("pe", lambda e, h4=h4, it=it, bT=bT, tv=tv: e.transpose(
                    tv, att_sb[:, h4, it, :], id_sb[:]),
                   reads=[("att", h4), ("id",), PBK[bT]], writes=[PBK[bT]])
                v = tv.rearrange("p (rl cp ce) -> p rl cp ce", rl=8, cp=8, ce=2)
                OP("vec", lambda e, h=h, it=it, v=v, bT=bT: e.tensor_copy(
                    AF2[0:64, :, 32 * h + 8 * it:32 * h + 8 * it + 8],
                    v[:, :, :, 0].transpose([0, 2, 1])),
                   reads=[PBK[bT]], writes=[("AF2", h, it, 0)])
                OP("vec", lambda e, h=h, it=it, v=v, bT=bT: e.tensor_copy(
                    AF2[64:128, :, 32 * h + 8 * it:32 * h + 8 * it + 8],
                    v[:, :, :, 1].transpose([0, 2, 1])),
                   reads=[PBK[bT]], writes=[("AF2", h, it, 1)])

        scores_part(0)
        for h in range(NH):
            if h + 1 < NH:
                scores_part(h + 1)
            umem_part(h)
            if h >= 2:
                attT_part(h - 2)
        attT_part(NH - 2)
        attT_part(NH - 1)

        af2_keys = [("AF2", h, it, ce) for h in range(NH) for it in range(4)
                    for ce in range(2)]

        for ot in range(4):
            op_ = ot % 2
            OP("sync", lambda e, t=t, ot=ot, op_=op_: e.dma_start(
                out=xsr[:, op_, :], in_=XdT[t, ot]),
               writes=[("xsl", op_)], dma=True)
            for dh in range(2):
                bk = 4 + dh
                for p8 in range(8):
                    OP("pe", lambda e, ot=ot, dh=dh, p8=p8, bk=bk: e.matmul(
                        pb[bk][:, :], AF2[:, p8, ot * 128:(ot + 1) * 128],
                        wo2_sb[:, p8, dh * 512:(dh + 1) * 512],
                        start=(p8 == 0), stop=(p8 == 7)),
                       reads=af2_keys + [("wos",), PBK[bk]], writes=[PBK[bk]])
                OP("vec", lambda e, dh=dh, bk=bk, op_=op_: e.tensor_copy(
                    ao_sb[:, op_, dh * 512:(dh + 1) * 512], pb[bk][:, :]),
                   reads=[PBK[bk]], writes=[("ao", op_)])
            if not dbg_raw_ao:
                ln_tile(ao_sb[:, op_, :], [("ao", op_)], op_,
                        ao_sb[:, op_, :], ("ao", op_))
                OP("vec", lambda e, op_=op_: e.tensor_add(
                    ao_sb[:, op_, :], ao_sb[:, op_, :], xsr[:, op_, :]),
                   reads=[("ao", op_), ("xsl", op_)], writes=[("ao", op_)])
            OP("sync", lambda e, ot=ot, op_=op_, t=t: e.dma_start(
                out=X1d[t * 512 + ot * 128:t * 512 + (ot + 1) * 128, :],
                in_=ao_sb[:, op_, :]),
               reads=[("ao", op_)], writes=[("x1d", t, ot)], dma=True)

    for t in range(n_segs):
        emit_seg(t)

    S.barrier()

    OP("vec", lambda e: e.memset(cumA[0:8, 0:512], 0.0), writes=[("cumA",)])
    OP("vec", lambda e: e.memset(cumB[0:8, 0:512], 0.0), writes=[("cumB",)])
    OP("vec", lambda e: e.memset(dg_sb[64:128, :, 1, :], 0.0),
       writes=[("dginit",)])

    WuV = Wu.ap().rearrange("E (kc p) (j half n) -> E p kc j half n",
                            p=128, j=16, half=2, n=128)
    WdV = Wd.ap().rearrange("E (hc p) n -> E p hc n", p=128)

    def emit_prologue(g):
        gp = g % 2
        for tt in range(NTT):
            tp = tt % 2
            r0 = g * TGRP + tt * 128
            OP("sync", lambda e, r0=r0, tp=tp: e.dma_start(
                out=x1t[:, tp, :], in_=X1d[r0:r0 + 128, :]),
               reads=[("x1d", r0 // 512, (r0 % 512) // 128)],
               writes=[("x1t", tp)], dma=True)
            ln_tile(x1t[:, tp, :], [("x1t", tp)], tp, yt_sb[:, tp, :], ("yt", tp))
            OP("vec", lambda e, tp=tp: e.bn_stats(
                stats[:, tp, 0:6], yt_sb[:, tp, 0:512]),
               reads=[("yt", tp)], writes=[("st_a", tp)])
            OP("vec", lambda e, tp=tp: e.bn_stats(
                stats[:, tp, 6:12], yt_sb[:, tp, 512:1024]),
               reads=[("yt", tp)], writes=[("st_b", tp)])
            OP("vec", lambda e, tp=tp: e.bn_aggr(
                mv[:, tp, :], stats[:, tp, :]),
               reads=[("st_a", tp), ("st_b", tp)], writes=[("mv", tp)])
            OP("vec", lambda e, tp=tp: e.tensor_scalar(
                out=rst[:, tp, 1:2], in0=mv[:, tp, 0:1],
                scalar1=mv[:, tp, 0:1], scalar2=None, op0=ALU.mult),
               reads=[("mv", tp)], writes=[("rt", tp)])
            OP("vec", lambda e, tp=tp: e.tensor_add(
                rst[:, tp, 1:2], rst[:, tp, 1:2], mv[:, tp, 1:2]),
               reads=[("rt", tp), ("mv", tp)], writes=[("rt", tp)])
            OP("act", lambda e, tp=tp: e.activation(
                out=rst[:, tp, 1:2], in_=rst[:, tp, 1:2],
                func=AF.Ln, scale=2.0 ** -40, bias=cst[:, 2:3]),
               reads=[("rt", tp), ("cst",)], writes=[("rt", tp)])
            OP("act", lambda e, tp=tp: e.activation(
                out=rst[:, tp, 0:1], in_=rst[:, tp, 1:2],
                func=AF.Exp, scale=-0.5, bias=cst[:, 4:5]),
               reads=[("rt", tp), ("cst",)], writes=[("rs", tp)])
            OP("vec", lambda e, tp=tp, tt=tt, gp=gp: e.tensor_scalar(
                out=xn_tok[:, gp, tt, :], in0=yt_sb[:, tp, :],
                scalar1=rst[:, tp, 0:1], scalar2=None, op0=ALU.mult),
               reads=[("yt", tp), ("rs", tp)], writes=[("xn", gp, tt)])
            for kc in range(8):
                bk = 6 + kc % 2
                OP("pe", lambda e, kc=kc, tt=tt, gp=gp, bk=bk: e.transpose(
                    pbb[bk][:, 0:128],
                    xn_tok[:, gp, tt, kc * 128:(kc + 1) * 128], id_sb[:]),
                   reads=[("xn", gp, tt), ("id",), PBK[bk]], writes=[PBK[bk]])
                OP("vec", lambda e, kc=kc, tp=tp, bk=bk: e.tensor_scalar(
                    out=xnTt[:, 0, kc, :],
                    in0=pbb[bk][:, 0:128], scalar1=sc_sb[:, kc:kc + 1],
                    scalar2=None, op0=ALU.mult),
                   reads=[PBK[bk], ("scs",)], writes=[("xnTt", kc)])
            for kc in range(8):
                OP("pe", lambda e, kc=kc, tp=tp: e.matmul(
                    pb[6][:, 256:264], xnTt[:, 0, kc, :],
                    wr_sb[:, kc, :], start=(kc == 0), stop=(kc == 7)),
                   reads=[("xnTt", kc), ("wrs",), PBK[6]],
                   writes=[PBK[6]])
            OP("vec", lambda e: e.tensor_copy(sc8[:, :], pb[6][:, 256:264]),
               reads=[PBK[6]], writes=[("sc8",)])
            OP("vec", lambda e: e.max(out=t8[:, :], in_=sc8[:, :]),
               reads=[("sc8",)], writes=[("t8",)])
            OP("act", lambda e: e.activation(
                out=r8[:, 1:2], in_=t8[:, 1:2], func=AF.Exp,
                bias=t8[:, 0:1], scale=-1.0),
               reads=[("t8",)], writes=[("r8b",)])
            OP("vec", lambda e: e.tensor_scalar(
                out=r8[:, 1:2], in0=r8[:, 1:2], scalar1=1.0, scalar2=None,
                op0=ALU.add),
               reads=[("r8b",)], writes=[("r8b",)])
            OP("vec", lambda e: e.reciprocal(r8[:, 2:3], r8[:, 1:2]),
               reads=[("r8b",)], writes=[("w2",)])
            OP("vec", lambda e: e.tensor_scalar(
                out=r8[:, 3:4], in0=r8[:, 2:3], scalar1=-1.0, scalar2=1.0,
                op0=ALU.mult, op1=ALU.add),
               reads=[("w2",)], writes=[("w1",)])
            OP("vec", lambda e: e.tensor_copy(r8[:, 4:5], t8[:, 1:2]),
               reads=[("t8",)], writes=[("m2c",)])
            OP("vec", lambda e, tt=tt, gp=gp: e.tensor_scalar(
                out=wm_sb[:, gp, tt, :], in0=sc8[:, :], scalar1=t8[:, 0:1],
                scalar2=r8[:, 3:4], op0=ALU.is_equal, op1=ALU.mult),
               reads=[("sc8",), ("t8",), ("w1",)], writes=[("wm", gp, tt)])
            OP("vec", lambda e: e.tensor_scalar(
                out=sc8[:, :], in0=sc8[:, :], scalar1=r8[:, 4:5],
                scalar2=r8[:, 2:3], op0=ALU.is_equal, op1=ALU.mult),
               reads=[("sc8",), ("m2c",), ("w2",)], writes=[("sc8",)])
            OP("vec", lambda e, tt=tt, gp=gp: e.tensor_add(
                wm_sb[:, gp, tt, :], wm_sb[:, gp, tt, :], sc8[:, :]),
               reads=[("wm", gp, tt), ("sc8",)], writes=[("wm", gp, tt)])

        for z in range(NTT // 4):
            OP("vec", lambda e, z=z, gp=gp: e.memset(
                accb[:, gp, 4 * z:4 * z + 4, :], 0.0),
               writes=[("accb", gp, z)])
        for sg in range(2):
            for t4 in range(4):
                tt = sg * 4 + t4
                OP("vec", lambda e, tt=tt, gp=gp: e.tensor_scalar(
                    out=m8_sb[:, :], in0=wm_sb[:, gp, tt, :], scalar1=0.0,
                    scalar2=None, op0=ALU.is_gt),
                   reads=[("wm", gp, tt)], writes=[("m8",)])
                tv = pbb[7][0:8, 896:1024]
                OP("pe", lambda e, tv=tv: e.transpose(tv, m8_sb[:, :],
                                                      id_sb[:]),
                   reads=[("m8",), ("id",), PBK[7]], writes=[PBK[7]])
                OP("vec", lambda e, t4=t4, tv=tv: e.tensor_copy(
                    cumA[0:8, 512 + t4 * 128:512 + (t4 + 1) * 128], tv),
                   reads=[PBK[7]], writes=[("cumA",)])
                OP("vec", lambda e, t4=t4, tv=tv: e.tensor_copy(
                    mT0[0:8, t4 * 128:(t4 + 1) * 128], tv),
                   reads=[PBK[7]], writes=[("mT0",)])
            src, dst = cumA, cumB
            sk_, dk_ = ("cumA",), ("cumB",)
            k = 1
            while k < 512:
                OP("vec", lambda e, src=src, dst=dst, k=k: e.tensor_add(
                    dst[0:8, 512:1024], src[0:8, 512:1024],
                    src[0:8, 512 - k:1024 - k]),
                   reads=[sk_], writes=[dk_])
                src, dst = dst, src
                sk_, dk_ = dk_, sk_
                k *= 2
            OP("vec", lambda e, src=src, sg=sg, gp=gp: e.tensor_tensor(
                out=slotTb[0:8, gp, sg * 512:(sg + 1) * 512],
                in0=src[0:8, 512:1024], in1=mT0[0:8, :], op=ALU.mult),
               reads=[sk_, ("mT0",)], writes=[("slotT", gp, sg)])
            OP("vec", lambda e, sg=sg, gp=gp: e.tensor_scalar(
                out=slotTb[0:8, gp, sg * 512:(sg + 1) * 512],
                in0=slotTb[0:8, gp, sg * 512:(sg + 1) * 512],
                scalar1=1.0, scalar2=None, op0=ALU.subtract),
               reads=[("slotT", gp, sg)], writes=[("slotT", gp, sg)])
            for t4 in range(4):
                tt = sg * 4 + t4
                tv2 = pbb[7][0:128, 896:904]
                OP("pe", lambda e, sg=sg, t4=t4, gp=gp, tv2=tv2: e.transpose(
                    tv2, slotTb[0:8, gp,
                                sg * 512 + t4 * 128:sg * 512 + (t4 + 1) * 128],
                    id_sb[0:8, 0:8]),
                   reads=[("slotT", gp, sg), ("id",), PBK[7]], writes=[PBK[7]])
                OP("vec", lambda e, tt=tt, gp=gp, tv2=tv2: e.tensor_copy(
                    slot_sb[:, gp, tt, :], tv2),
                   reads=[PBK[7]], writes=[("slot", gp, tt)])

    def emit_wd_load(ex):
        for c4 in range(4):
            OP("pool", lambda e, ex=ex, c4=c4: e.dma_start(
                out=wd_sb[:, 4 * c4:4 * c4 + 4, :],
                in_=WdV[ex, :, 4 * c4:4 * c4 + 4, :]),
               writes=[("wd", c4)], dma=True)

    def emit_expert(g, ex):
        gp = g % 2
        exp2 = ex % 2
        for sg in range(2):
            for t4 in range(4):
                tt = sg * 4 + t4
                OP("vec", lambda e, t4=t4, tt=tt, gp=gp: e.tensor_scalar(
                    out=STe[:, t4, :], in0=iotaC[:, :],
                    scalar1=slot_sb[:, gp, tt, ex:ex + 1],
                    scalar2=None, op0=ALU.is_equal),
                   reads=[("iotaC",), ("slot", gp, tt)],
                   writes=[("STe", t4)])
            gb = 2 + sg
            for dt in range(8):
                for t4 in range(4):
                    tt = sg * 4 + t4
                    OP("pe", lambda e, dt=dt, t4=t4, tt=tt, gp=gp, gb=gb:
                       e.matmul(
                           pb[gb][:, 0:224],
                           xn_tok[:, gp, tt, dt * 128:(dt + 1) * 128],
                           STe[:, t4, :], start=(t4 == 0), stop=(t4 == 3)),
                       reads=[("xn", gp, tt), ("STe", t4), PBK[gb]],
                       writes=[PBK[gb]])
                OP("vec", lambda e, dt=dt, sg=sg, gb=gb: e.tensor_scalar(
                    out=xgT[:, sg, dt, :], in0=pb[gb][:, 0:224],
                    scalar1=sc_sb[:, dt:dt + 1], scalar2=None, op0=ALU.mult),
                   reads=[PBK[gb], ("scs",)], writes=[("xgT", sg, dt)])
            OP("pe", lambda e, ex=ex, sg=sg, gp=gp: e.matmul(
                pb[sg][:, 0:512], e8_sb[0:8, ex, :],
                slotTb[0:8, gp, sg * 512:(sg + 1) * 512],
                start=True, stop=True),
               reads=[("e8",), ("slotT", gp, sg), PBK[sg]], writes=[PBK[sg]])
            for ck in range(2):
                OP("vec", lambda e, sg=sg, ck=ck: e.tensor_scalar(
                    out=SSb[:, sg, ck, :], in0=pb[sg][:, 0:512],
                    scalar1=iota2[:, ck:ck + 1], scalar2=None,
                    op0=ALU.is_equal),
                   reads=[PBK[sg], ("iota2",)], writes=[("SS", sg, ck)])
        xg_keys = [("xgT", sg, dt) for sg in range(2) for dt in range(8)]
        for j in range(16):
            lb = j % 3
            OP("pool", lambda e, ex=ex, j=j, lb=lb: e.dma_start(
                out=wu_sb[:, lb, :, :],
                in_=WuV[ex, :, :, j, :, :].rearrange(
                    "p kc half n -> p kc (half n)")),
               writes=[("wu", lb)], dma=True)
            for half in range(2):
                for sg in range(2):
                    bk = half * 4 + sg
                    for kc in range(8):
                        OP("pe", lambda e, j=j, lb=lb, half=half, sg=sg,
                           kc=kc, bk=bk: e.matmul(
                               pb[bk][:, 0:224],
                               wu_sb[:, lb, kc, half * 128:half * 128 + 128],
                               xgT[:, sg, kc, :],
                               start=(kc == 0), stop=(kc == 7)),
                           reads=[("wu", lb), ("xgT", sg, kc), PBK[bk]],
                           writes=[PBK[bk]])
            for sg in range(2):
                OP("act", lambda e, sg=sg: e.activation(
                    out=sg_sb[:, sg, :], in_=pb[4 + sg][:, 0:224],
                    func=AF.Silu),
                   reads=[PBK[4 + sg]], writes=[("sg", sg)])
                OP("vec", lambda e, j=j, sg=sg: e.tensor_mul(
                    hg_sb[:, sg, j, :], pb[sg][:, 0:224], sg_sb[:, sg, :]),
                   reads=[PBK[sg], ("sg", sg)], writes=[("hg", sg, j)])
        for sg in range(2):
            hg_keys = [("hg", sg, j) for j in range(16)]
            for dh in range(2):
                for sl in range(2):
                    bk = 2 + sl
                    w = 128 if sl == 0 else 96
                    for hc in range(16):
                        OP("pe", lambda e, sg=sg, dh=dh, sl=sl, hc=hc, bk=bk,
                           w=w: e.matmul(
                               pb[bk][0:w, :],
                               hg_sb[:, sg, hc, sl * 128:sl * 128 + w],
                               wd_sb[:, hc, dh * 512:(dh + 1) * 512],
                               start=(hc == 0), stop=(hc == 15)),
                           reads=hg_keys + [("wd", hc // 4), PBK[bk]],
                           writes=[PBK[bk]])
                    OP("vec", lambda e, sg=sg, dh=dh, sl=sl, bk=bk, w=w:
                       e.tensor_copy(
                           dg_sb[0:w, sg, sl, dh * 512:(dh + 1) * 512],
                           pb[bk][0:w, :]),
                       reads=[PBK[bk]], writes=[("dg", sg, sl, dh)])
            dg_keys = [("dg", sg, sl, dh) for sl in range(2) for dh in range(2)]
            for t4 in range(4):
                tt = sg * 4 + t4
                for dh in range(2):
                    bk = 6 + dh
                    for ck in range(2):
                        OP("pe", lambda e, sg=sg, t4=t4, dh=dh, ck=ck, bk=bk:
                           e.matmul(
                               pb[bk][:, :],
                               SSb[:, sg, ck, t4 * 128:(t4 + 1) * 128],
                               dg_sb[:, sg, ck, dh * 512:(dh + 1) * 512],
                               start=(ck == 0), stop=(ck == 1)),
                           reads=[("SS", sg, ck), ("dginit",)] + dg_keys
                           + [PBK[bk]],
                           writes=[PBK[bk]])
                    OP("vec", lambda e, tt=tt, dh=dh, bk=bk, ex=ex, gp=gp:
                       e.scalar_tensor_tensor(
                           out=accb[:, gp, tt, dh * 512:(dh + 1) * 512],
                           in0=pb[bk][:, :], scalar=wm_sb[:, gp, tt, ex:ex + 1],
                           in1=accb[:, gp, tt, dh * 512:(dh + 1) * 512],
                           op0=ALU.mult, op1=ALU.add),
                       reads=[PBK[bk], ("wm", gp, tt), ("accb", gp, tt // 4)],
                       writes=[("accb", gp, tt // 4)])

    def emit_epilogue(g):
        gp = g % 2
        for tt in range(NTT):
            tp = 2 + tt % 2
            bi = tt % 2
            r0 = g * TGRP + tt * 128
            OP("sync", lambda e, r0=r0, bi=bi: e.dma_start(
                out=x1t[:, bi, :], in_=X1d[r0:r0 + 128, :]),
               writes=[("x1t", bi)], dma=True)
            ln_tile(x1t[:, bi, :], [("x1t", bi)], tp, yt_sb[:, bi, :],
                    ("yt", bi))
            OP("vec", lambda e, tt=tt, bi=bi, gp=gp: e.tensor_add(
                s_sb[:, 0, :], yt_sb[:, bi, :], accb[:, gp, tt, :]),
               reads=[("yt", bi), ("accb", gp, tt // 4)], writes=[("s",)])
            ln_tile(s_sb[:, 0, :], [("s",)], tp, s_sb[:, 0, :],
                    ("s",))
            OP("vec", lambda e, bi=bi: e.tensor_add(
                s_sb[:, 0, :], s_sb[:, 0, :], x1t[:, bi, :]),
               reads=[("s",), ("x1t", bi)], writes=[("s",)])
            OP("sync", lambda e, r0=r0: e.dma_start(
                out=Yd[r0:r0 + 128, :], in_=s_sb[:, 0, :]),
               reads=[("s",)], writes=[("yout", r0)], dma=True)

    if n_groups:
        emit_prologue(0)
        emit_wd_load(0)
        for g in range(n_groups):
            for ex in range(NEXP):
                emit_expert(g, ex)
                if ex + 1 < NEXP:
                    emit_wd_load(ex + 1)
                elif g + 1 < n_groups:
                    emit_wd_load(0)
                if ex == 0 and g >= 1:
                    emit_epilogue(g - 1)
                if ex == 2 and g + 1 < n_groups:
                    emit_prologue(g + 1)
        emit_epilogue(n_groups - 1)

    if dbg_taps:
        tapmap = {
            "T1": (T1, [("T1q", c) for c in range(16)] +
                   [("T1sq", rc) for rc in range(4)], BF16),
            "T2": (T2, [("T2k", c) for c in range(16)], BF16),
            "lnT": (lnT, [("lnT", kc, i) for kc in range(8) for i in range(4)],
                    BF16),
            "es": (es_sb, [("es", h2, jt) for h2 in range(2) for jt in range(4)], BF16),
            "vtk": (vtk, [("vtk", 0), ("vtk", 1), ("vtk1",)], BF16),
            "sktk": (sktk, [("sktk", 0), ("sktk", 1)], BF16),
            "AF2": (AF2, [("AF2", h, it, ce) for h in range(NH)
                          for it in range(4) for ce in range(2)], BF16),
            "Msb": (M_sb, [("Msb",)], F32),
            "G": (G_sb, [("G",)], F32),
            "e0": (e0_sb, [("e0", 0), ("e0", 1)], BF16),
            "e1": (e1_sb, [("e1", 0), ("e1", 1)], BF16),
            "att": (att_sb, [], BF16),
            "rc": (rc_sb, [("rc", 0), ("rc", 1)], F32),
            "ao": (ao_sb, [("ao", 0), ("ao", 1)], F32),
            "xnT": (xnT, [], BF16),
            "hT": (hT, [("hT", j) for j in range(16)], BF16),
            "accb": (accb, [], BF16),
            "wm": (wm_sb, [], F32),
        }
        for name in dbg_taps:
            h, keys, dt = tapmap[name]
            shp = list(h.shape)
            dram = nc.dram_tensor("tap_" + name, shp, dt, kind="ExternalOutput")
            if dt == F32:
                OP("sync", lambda e, dram=dram, h=h: e.dma_start(
                    out=dram.ap(), in_=h.ap()), reads=keys, dma=True)
            else:
                OP("pool", lambda e, dram=dram, h=h: e.dma_start(
                    out=dram.ap(), in_=h.ap()), reads=keys, dma=True)

    semkeys = set()
    for o in S.ops:
        if o.dma:
            for ln in range(Sched.NLANES):
                semkeys.add((o.stream, True, ln))
        else:
            semkeys.add((o.stream, False))
    sems = {}
    for sk in semkeys:
        nm = (f"sem_{sk[0]}_d{sk[2]}" if len(sk) == 3 else f"sem_{sk[0]}_c")
        cm = nc.semaphore(nm)
        sems[sk] = cm.__enter__()
    S.finalize_and_emit(sems)
    nc._sched = S
    return nc


def _host_prep(inputs):
    x = np.ascontiguousarray(inputs["L0_attn_x"], dtype=np.float32)
    betas = np.asarray(inputs["L0_attn_betas"], dtype=np.float32).reshape(NH, DK)
    betab = np.ascontiguousarray(
        np.broadcast_to(betas.reshape(1, NH * DK), (128, NH * DK))).astype(np.float32)
    m0 = np.asarray(inputs["L0_attn_memory0"], dtype=np.float32).reshape(NH, DK, DK)
    z0 = np.asarray(inputs["L0_attn_z0"], dtype=np.float32).reshape(NH, DK)
    maug0 = np.empty((DK, NH, 65), np.float32)
    maug0[:, :, 0:64] = m0.transpose(1, 0, 2)
    maug0[:, :, 64] = z0.transpose(1, 0)
    nsc = np.ascontiguousarray(
        np.asarray(inputs["L0_ffn_norm_scale"], dtype=np.float32).reshape(8, 128).T)
    wrT = np.ascontiguousarray(
        np.asarray(inputs["L0_ffn_w_router"], dtype=np.float32).T)
    wut = np.asarray(inputs["L0_ffn_w_up"], dtype=np.float32).transpose(0, 2, 1)
    wut = wut.reshape(NEXP, D, 2, 16, 128).transpose(0, 1, 3, 2, 4)
    wut = np.ascontiguousarray(wut.reshape(NEXP, D, 2 * HID))
    wdt = np.ascontiguousarray(
        np.asarray(inputs["L0_ffn_w_down"], dtype=np.float32).transpose(0, 2, 1))
    common = {
        "wq": np.ascontiguousarray(inputs["L0_attn_Wq"], dtype=np.float32),
        "wk": np.ascontiguousarray(inputs["L0_attn_Wk"], dtype=np.float32),
        "wv": np.ascontiguousarray(inputs["L0_attn_Wv"], dtype=np.float32),
        "wo": np.ascontiguousarray(inputs["L0_attn_Wo"], dtype=np.float32),
        "betab": betab, "maug0": maug0, "nsc": nsc,
        "wr": wrT, "wu": wut, "wd": wdt,
    }
    return [dict(common, x=np.ascontiguousarray(x[c])) for c in range(8)]


def kernel(L0_attn_x, L0_attn_Wq, L0_attn_Wk, L0_attn_Wv, L0_attn_Wo,
           L0_attn_betas, L0_attn_memory0, L0_attn_z0, L0_ffn_norm_scale,
           L0_ffn_w_router, L0_ffn_w_up, L0_ffn_w_down):
    inputs = dict(
        L0_attn_x=L0_attn_x, L0_attn_Wq=L0_attn_Wq, L0_attn_Wk=L0_attn_Wk,
        L0_attn_Wv=L0_attn_Wv, L0_attn_Wo=L0_attn_Wo, L0_attn_betas=L0_attn_betas,
        L0_attn_memory0=L0_attn_memory0, L0_attn_z0=L0_attn_z0,
        L0_ffn_norm_scale=L0_ffn_norm_scale, L0_ffn_w_router=L0_ffn_w_router,
        L0_ffn_w_up=L0_ffn_w_up, L0_ffn_w_down=L0_ffn_w_down)
    in_maps = _host_prep(inputs)
    nc = build_program()
    res = run_bass_kernel_spmd(nc, in_maps, core_ids=list(range(8)))
    return np.stack([np.asarray(r["y"], dtype=np.float32) for r in res.results])
